# version 13
# baseline (speedup 1.0000x reference)
import numpy as np
import concourse.bass as bass
import concourse.mybir as mybir
from concourse.bass_utils import run_bass_kernel_spmd
from concourse.alu_op_type import AluOpType as Op

F32 = mybir.dt.float32
AF = mybir.ActivationFunctionType
AX = mybir.AxisListType

D = 1024
S = 2048
DFF = 4096
KD = D // 128
KF = DFF // 128
T4 = 512
NT4 = S // T4
T8 = 256
NT8 = S // T8
RMS_EPS = 1e-6
LN_EPS = 1e-5
NEG = -1.0e30

VC = {}
_c = 0
for _name, _n in [("bg", 8), ("bv", 8), ("bd", 8), ("wm", 8), ("wa", 8), ("wf", 8),
                  ("s1", 32), ("s2", 32), ("sh1", 32), ("sh2", 32), ("s3", 8),
                  ("eps1", 1), ("eps2", 1)]:
    VC[_name] = (_c, _c + _n)
    _c += _n
VECN = _c

CC = {"ident": (0, 128), "ones": (128, 256), "mask": (256, 512), "mask0": (512, 768)}
CONN = 768


class Sem:
    def __init__(self, nc, name):
        self.name = name
        self.h = nc.alloc_semaphore(name)
        self.n = 0

    def inc(self, instr, k=1):
        instr.then_inc(self.h, k)
        self.n += k
        return self.n


class Prog:
    def __init__(self, nc):
        self.nc = nc
        self.sems = {s: Sem(nc, "s" + s) for s in
                     ["LW0", "LW1", "LW2", "LW3", "LX", "LR0", "LR1", "MM", "VE",
                      "SE", "FW", "ST", "CA", "CB", "CD0", "CD1", "CE", "CI0", "CI1", "CI2", "CI3"]}
        self.last_wait = {}
        self.bank_guard = [None] * 8

    def w(self, eng, eng_name, sem, val):
        if val is None or val <= 0:
            return
        key = (eng_name, sem.name)
        if self.last_wait.get(key, -1) >= val:
            return
        eng.wait_ge(sem.h, val)
        self.last_wait[key] = val


DEBUG_DUMPS = False


def build_program():
    nc = bass.Bass("TRN2", target_bir_lowering=False)

    xT = nc.dram_tensor("xT", [KD, 128, S], F32, kind="ExternalInput").ap()
    yT = nc.dram_tensor("yT", [KD, 128, S], F32, kind="ExternalOutput").ap()
    wg = nc.dram_tensor("wg", [8, 128, D], F32, kind="ExternalInput").ap()
    wv = nc.dram_tensor("wv", [8, 128, D], F32, kind="ExternalInput").ap()
    wd = nc.dram_tensor("wd", [8, 128, D], F32, kind="ExternalInput").ap()
    wqkv = nc.dram_tensor("wqkv", [24, 128, D], F32, kind="ExternalInput").ap()
    wo = nc.dram_tensor("wo", [8, 128, D], F32, kind="ExternalInput").ap()
    wok = nc.dram_tensor("wok", [8, 128, D], F32, kind="ExternalInput").ap()
    wgate = nc.dram_tensor("wgate", [32, 128, D], F32, kind="ExternalInput").ap()
    wup = nc.dram_tensor("wup", [32, 128, D], F32, kind="ExternalInput").ap()
    wout = nc.dram_tensor("wout", [8, 128, DFF], F32, kind="ExternalInput").ap()
    f1 = nc.dram_tensor("f1", [32, 128, D], F32, kind="ExternalInput").ap()
    f2 = nc.dram_tensor("f2", [32, 128, D], F32, kind="ExternalInput").ap()
    fh1 = nc.dram_tensor("fh1", [32, 128, DFF], F32, kind="ExternalInput").ap()
    fh2 = nc.dram_tensor("fh2", [32, 128, DFF], F32, kind="ExternalInput").ap()
    f3 = nc.dram_tensor("f3", [8, 128, DFF], F32, kind="ExternalInput").ap()
    dbg = {}
    if DEBUG_DUMPS:
        for tag in ["s1", "s2", "s3", "vv", "ww", "x2f"]:
            dbg[tag] = nc.dram_tensor("dbg_" + tag, [KD, 128, S], F32,
                                      kind="ExternalOutput").ap()
        dbg["qkv"] = nc.dram_tensor("dbg_qkv", [128, 6400], F32,
                                    kind="ExternalOutput").ap()
        dbg["prb"] = nc.dram_tensor("dbg_prb", [128, 2048], F32,
                                    kind="ExternalOutput").ap()
        dbg["prt"] = nc.dram_tensor("dbg_prt", [128, 512], F32,
                                    kind="ExternalOutput").ap()
        dbg["vtb"] = nc.dram_tensor("dbg_vtb", [128, 256], F32,
                                    kind="ExternalOutput").ap()
        dbg["avt"] = nc.dram_tensor("dbg_avt", [128, 512], F32,
                                    kind="ExternalOutput").ap()
    FP8 = mybir.dt.float8e4
    BF16 = mybir.dt.bfloat16
    c1 = nc.dram_tensor("c1", [32, 128, D], FP8, kind="Internal").ap()
    c2 = nc.dram_tensor("c2", [32, 128, D], FP8, kind="Internal").ap()
    ch1 = nc.dram_tensor("ch1", [32, 128, DFF], FP8, kind="Internal").ap()
    ch2 = nc.dram_tensor("ch2", [32, 128, DFF], FP8, kind="Internal").ap()
    c3 = nc.dram_tensor("c3", [8, 128, DFF], FP8, kind="Internal").ap()
    pqkv = nc.dram_tensor("pqkv", [24, 128, 2 * D], BF16, kind="Internal").ap()
    pg = nc.dram_tensor("pg", [8, 128, 2 * D], BF16, kind="Internal").ap()
    pv = nc.dram_tensor("pv", [8, 128, 2 * D], BF16, kind="Internal").ap()
    pd = nc.dram_tensor("pd", [8, 128, 2 * D], BF16, kind="Internal").ap()
    pgate = nc.dram_tensor("pgate", [32, 128, 2 * D], BF16, kind="Internal").ap()
    pup = nc.dram_tensor("pup", [32, 128, 2 * D], BF16, kind="Internal").ap()
    pout = nc.dram_tensor("pout", [8, 128, 2 * DFF], BF16, kind="Internal").ap()
    vecs = nc.dram_tensor("vecs", [128, VECN], F32, kind="ExternalInput").ap()
    consts = nc.dram_tensor("consts", [128, CONN], F32, kind="ExternalInput").ap()

    X = nc.alloc_sbuf_tensor("X", [128, KD * S], F32).ap()
    SCR = nc.alloc_sbuf_tensor("SCR", [128, 16384], F32).ap()
    QKV = nc.alloc_sbuf_tensor("QKV", [128, 2048 + 2 * 2176], F32).ap()
    PRB = nc.alloc_sbuf_tensor("PRB", [128, 2048], F32).ap()
    PRT = nc.alloc_sbuf_tensor("PRT", [128, 512], F32).ap()
    VTB = nc.alloc_sbuf_tensor("VTB", [128, 256], F32).ap()
    AVT = nc.alloc_sbuf_tensor("AVT", [128, 512], F32).ap()
    XSQ = nc.alloc_sbuf_tensor("XSQ", [128, 2048], F32).ap()
    TMP1 = nc.alloc_sbuf_tensor("TMP1", [128, 512], F32).ap()
    TMP2 = nc.alloc_sbuf_tensor("TMP2", [128, 512], F32).ap()
    TMP3 = nc.alloc_sbuf_tensor("TMP3", [128, 512], F32).ap()
    WS = nc.alloc_sbuf_tensor("WS", [128, 4096], F32).ap()
    XR = nc.alloc_sbuf_tensor("XR", [128, 1024], F32).ap()
    VEC = nc.alloc_sbuf_tensor("VEC", [128, VECN], F32).ap()
    CON = nc.alloc_sbuf_tensor("CON", [128, CONN], F32).ap()
    CAR = nc.alloc_sbuf_tensor("CAR", [128, 8], F32).ap()
    RWS = nc.alloc_sbuf_tensor("RWS", [128, 16], F32).ap()
    LNT = nc.alloc_sbuf_tensor("LNT", [128, 1024], F32).ap()

    PS = [nc.alloc_psum_tensor("psb%d" % i, [128, 512], F32).ap() for i in range(8)]

    QOFF = 0
    KOFF = 2048
    VOFF = 2048 + 2176

    p = Prog(nc)
    (LW0, LW1, LW2, LW3, LX, LR0, LR1, MM, VE, SE, FW, ST, CA, CB, CD0, CD1,
     CE, CI0, CI1, CI2, CI3) = (p.sems[k]
        for k in ["LW0", "LW1", "LW2", "LW3", "LX", "LR0", "LR1", "MM", "VE", "SE",
                  "FW", "ST", "CA", "CB", "CD0", "CD1", "CE", "CI0", "CI1", "CI2",
                  "CI3"])
    CIh = [CI0, CI1, CI2, CI3]
    CDh = [CD0, CD1]
    LWh = [LW0, LW1, LW2, LW3]
    LRh = [LR0, LR1]

    def vec(name, j):
        a, b = VC[name]
        assert a + j < b
        return VEC[:, a + j:a + j + 1]

    def xch(m, t0=0, t1=S):
        return X[:, m * S + t0: m * S + t1]

    with nc.Block() as block:
        @block.sync
        def _(e):
            r = e.dma_start(out=VEC[:], in_=vecs[:])
            LX.inc(r, 16)
            r = e.dma_start(out=CON[:], in_=consts[:])
            LX.inc(r, 16)
            for m in range(KD):
                r = e.dma_start(out=xch(m), in_=xT[m])
                LX.inc(r, 16)
        aux_loaded = LX.n
        x_loaded = LX.n

        @block.vector
        def _(v):
            r = v.memset(VTB[:], 0.0)
            VE.inc(r, 1)
        pads_zeroed = VE.n

        @block.scalar
        def _(s):
            r = s.memzero(CAR[:])
            SE.inc(r, 1)
        carry_ready = SE.n

        QKVBc = QKV.bitcast(mybir.dt.bfloat16)
        SCRc = SCR.bitcast(mybir.dt.bfloat16)
        cache_marks = []

        @block.gpsimd
        def _(g):
            jobs = []
            for srcw, dstw, nm in [(wg, pg, 8), (wv, pv, 8), (wd, pd, 8),
                                   (wqkv, pqkv, 24), (wgate, pgate, 32),
                                   (wup, pup, 32)]:
                for m in range(0, nm, 2):
                    jobs.append((
                        srcw[m:m + 2].rearrange("a p k -> p a k"),
                        True,
                        [(dstw[m], 0, 2048), (dstw[m + 1], 2048, 4096)]))
            for m in range(8):
                for kh in range(2):
                    jobs.append((
                        wout[m][:, kh * 2048:(kh + 1) * 2048],
                        False,
                        [(pout[m][:, kh * 4096:(kh + 1) * 4096], 0, 4096)]))
            cb32 = [QKV[:, 0:2048], QKV[:, 2048:4096]]
            cbhl = [SCRc[:, 24576:28672], SCRc[:, 28672:32768]]
            buf_free = [(None, 0), (None, 0)]
            in_done = []
            def in_ap(b, job):
                if job[1]:
                    return cb32[b][:].rearrange("p (a k) -> p a k", a=2)
                return cb32[b][:]

            for j in range(min(2, len(jobs))):
                r = g.dma_start(out=in_ap(j % 2, jobs[j]), in_=jobs[j][0])
                CIh[j % 2].inc(r, 16)
                in_done.append((CIh[j % 2], CIh[j % 2].n))
            for j, (srcap, inter, outdmas) in enumerate(jobs):
                b = j % 2
                p.w(g, "gpsimd", in_done[j][0], in_done[j][1])
                sem2, val2 = buf_free[b]
                if sem2 is not None:
                    p.w(g, "gpsimd", sem2, val2)
                if inter:
                    cout = cbhl[b][:].rearrange("p (a two k) -> p a two k",
                                                a=2, two=2)[:, :, 0, :]
                    lout = cbhl[b][:].rearrange("p (a two k) -> p a two k",
                                                a=2, two=2)[:, :, 1, :]
                    cin = cb32[b][:].rearrange("p (a k) -> p a k", a=2)
                else:
                    cout = cbhl[b][:, 0:2048]
                    lout = cbhl[b][:, 2048:4096]
                    cin = cb32[b][:]
                r = g.dma_start(out=cout, in_=cin)
                CA.inc(r, 16)
                p.w(g, "gpsimd", CA, CA.n)
                r = g.tensor_tensor(lout, cin, cout, Op.subtract)
                CB.inc(r, 1)
                p.w(g, "gpsimd", CB, CB.n)
                for dst, a0, a1 in outdmas:
                    r = g.dma_start(out=dst, in_=cbhl[b][:, a0:a1])
                    CDh[b].inc(r, 16)
                buf_free[b] = (CDh[b], CDh[b].n)
                cache_marks.append((CDh[b], CDh[b].n))
                jn = j + 2
                if jn < len(jobs):
                    semf, valf = buf_free[b]
                    p.w(g, "gpsimd", semf, valf)
                    r = g.dma_start(out=in_ap(b, jobs[jn]), in_=jobs[jn][0])
                    CIh[jn % 2].inc(r, 16)
                    in_done.append((CIh[jn % 2], CIh[jn % 2].n))
            for srcw, dstw, nm in [(f1, c1, 32), (f2, c2, 32), (fh1, ch1, 32),
                                   (fh2, ch2, 32), (f3, c3, 8)]:
                for m in range(nm):
                    r = g.dma_start(out=dstw[m], in_=srcw[m])
                    CE.inc(r, 16)
        cache_done = CE.n
        dense_last = [cache_marks[-1], cache_marks[-2]]

        ident = CON[:, CC["ident"][0]:CC["ident"][1]]
        ones = CON[:, CC["ones"][0]:CC["ones"][1]]
        mask_g = CON[:, CC["mask"][0]:CC["mask"][1]]
        mask_0 = CON[:, CC["mask0"][0]:CC["mask0"][1]]

        ws_free = [(None, 0), (None, 0)]
        ws_loaded = [0, 0]

        def load_slab(src_ap, ncols, half):
            @block.sync
            def _(e):
                sem, val = ws_free[half]
                if sem is not None:
                    p.w(e, "sync", sem, val)
                r = e.dma_start(out=WS[:, half * 2048: half * 2048 + ncols], in_=src_ap)
                LWh[half].inc(r, 16)
            ws_loaded[half] = (LWh[half], LWh[half].n)

        def slab(half, k):
            return WS[:, half * 2048 + k * 128: half * 2048 + (k + 1) * 128]

        q_free = [(None, 0)] * 4
        q_loaded = [0] * 4

        def load_q(src_ap, q):
            @block.sync
            def _(e):
                sem, val = q_free[q]
                if sem is not None:
                    p.w(e, "sync", sem, val)
                r = e.dma_start(out=WS[:, q * 1024:(q + 1) * 1024], in_=src_ap)
                LWh[q].inc(r, 16)
            q_loaded[q] = (LWh[q], LWh[q].n)

        def slab_q(q, k):
            return WS[:, q * 1024 + k * 128: q * 1024 + (k + 1) * 128]

        WSB = WS.bitcast(mybir.dt.bfloat16)

        def slab_hi(q, k):
            return WSB[:, q * 2048 + k * 128: q * 2048 + (k + 1) * 128]

        def slab_lo(q, k):
            return WSB[:, q * 2048 + 1024 + k * 128: q * 2048 + 1024 + (k + 1) * 128]

        def load_pair(src_ap, ncols2, q, marks=None):
            @block.sync
            def _(e):
                sem, val = q_free[q]
                if sem is not None:
                    p.w(e, "sync", sem, val)
                for semx, valx in (marks if marks is not None else dense_last):
                    p.w(e, "sync", semx, valx)
                r = e.dma_start(out=WSB[:, q * 2048: q * 2048 + ncols2], in_=src_ap)
                LWh[q].inc(r, 16)
            q_loaded[q] = (LWh[q], LWh[q].n)

        WS8 = WS.bitcast(mybir.dt.float8e4)

        def load_slab8(src_ap, ncols, half):
            @block.sync
            def _(e):
                sem, val = ws_free[half]
                if sem is not None:
                    p.w(e, "sync", sem, val)
                p.w(e, "sync", CE, cache_done)
                r = e.dma_start(out=WS8[:, half * 8192: half * 8192 + ncols],
                                in_=src_ap)
                LWh[half].inc(r, 16)
            ws_loaded[half] = (LWh[half], LWh[half].n)

        def slab8(half, k):
            return WS8[:, half * 8192 + k * 128: half * 8192 + (k + 1) * 128]

        def bank_wait(t, b):
            g = p.bank_guard[b]
            if g is not None:
                p.w(t, "tensor", g[0], g[1])

        sq_consumed = {0: 0, 1: 0, 2: 0, 3: 0}
        dump_gate = [0]

        def dump_x(tag, ve_val, src_base=None):
            if not DEBUG_DUMPS:
                return
            @block.gpsimd
            def _(g):
                p.w(g, "gpsimd", VE, ve_val)
                for m in range(KD):
                    if src_base is None:
                        r = g.dma_start(out=dbg[tag][m], in_=xch(m))
                    else:
                        r = g.dma_start(out=dbg[tag][m],
                                        in_=SCR[:, m * 2048:(m + 1) * 2048])
                    ST.inc(r, 16)
            dump_gate[0] = ST.n

        def dump_buf(tag, src_ap, sem, val):
            if not DEBUG_DUMPS:
                return
            @block.gpsimd
            def _(g):
                p.w(g, "gpsimd", sem, val)
                r = g.dma_start(out=dbg[tag], in_=src_ap)
                ST.inc(r, 16)
            dump_gate[0] = ST.n

        def rms_pass(tt, bankQ, after_ve=0):
            mmv = 0
            for m in range(KD):
                half = m % 4
                @block.scalar
                def _(s):
                    p.w(s, "scalar", VE, after_ve)
                    p.w(s, "scalar", LX, x_loaded)
                    p.w(s, "scalar", MM, sq_consumed[half])
                    r = s.activation(XSQ[:, half * 512:(half + 1) * 512],
                                     xch(m, tt * T4, (tt + 1) * T4), AF.Square)
                    SE.inc(r, 1)
                se_val = SE.n

                @block.tensor
                def _(t):
                    p.w(t, "tensor", SE, se_val)
                    p.w(t, "tensor", LX, aux_loaded)
                    if m == 0:
                        bank_wait(t, bankQ)
                    r = t.matmul(PS[bankQ][:], ones, XSQ[:, half * 512:(half + 1) * 512],
                                 start=(m == 0), stop=(m == KD - 1))
                    MM.inc(r, 1)
                sq_consumed[half] = MM.n
                mmv = MM.n
            return mmv

        rstd_tmp1_ve = {0: 0, 1: 0}

        def rstd_from_bank(tt, bankQ, mm_val, eps, dest):
            TMPs = TMP1 if tt % 2 == 0 else PRT
            @block.scalar
            def _(s):
                p.w(s, "scalar", MM, mm_val)
                p.w(s, "scalar", VE, rstd_tmp1_ve[tt % 2])
                r = s.activation(TMPs[:], PS[bankQ][:], AF.Sqrt, bias=eps,
                                 scale=1.0 / D)
                SE.inc(r, 1)
            se_val = SE.n
            p.bank_guard[bankQ] = (SE, se_val)

            @block.vector
            def _(v):
                p.w(v, "vector", SE, se_val)
                r = v.reciprocal(dest[:, tt * T4:(tt + 1) * T4], TMPs[:])
                VE.inc(r, 1)
            rstd_tmp1_ve[tt % 2] = VE.n
            return VE.n

        rstd_ve = [0] * NT4
        for tt in range(NT4):
            mmv = rms_pass(tt, 6 + tt % 2)
            rstd_ve[tt] = rstd_from_bank(tt, 6 + tt % 2, mmv, vec("eps1", 0), PRB)

        SCRX = SCR.bitcast(mybir.dt.bfloat16)
        x1_prev_hi = {0: 0, 1: 0}
        last_gvd_se = [0]
        last_scan_ve = [0]
        mingru_marks = [cache_marks[3 * 4 + 0], cache_marks[3 * 4 - 1]]

        def x1hi(k):
            return SCRX[:, k * 1024: k * 1024 + 512]

        def x1lo(k):
            return SCRX[:, k * 1024 + 512: (k + 1) * 1024]

        for tt in range(NT4):
            for k in range(KD):
                TMPa = TMP2 if k % 2 == 0 else TMP3
                @block.vector
                def _(v):
                    p.w(v, "vector", VE, rstd_ve[tt])
                    p.w(v, "vector", LX, aux_loaded)
                    p.w(v, "vector", SE, x1_prev_hi[k % 2])
                    p.w(v, "vector", SE, last_gvd_se[0])
                    p.w(v, "vector", SE, carry_ready)
                    p.w(v, "vector", VE, last_scan_ve[0])
                    r = v.scalar_tensor_tensor(TMPa[:],
                        xch(k, tt * T4, (tt + 1) * T4), vec("wm", k),
                        PRB[:, tt * T4:(tt + 1) * T4], Op.mult, Op.mult)
                    VE.inc(r, 1)
                ve_f = VE.n

                @block.scalar
                def _(s):
                    p.w(s, "scalar", VE, ve_f)
                    r = s.activation(x1hi(k), TMPa[:], AF.Copy, bias=0.0)
                    SE.inc(r, 1)
                hi_se = SE.n
                x1_prev_hi[k % 2] = hi_se

                @block.vector
                def _(v):
                    p.w(v, "vector", SE, hi_se)
                    r = v.tensor_tensor(x1lo(k), TMPa[:], x1hi(k), Op.subtract)
                    VE.inc(r, 1)
            x1c_ready = VE.n
            mingru_lwv = {}

            mp_src = [pg, pv, pd]
            for m in range(KD):
                bs = 0 if m % 2 == 0 else 3
                j0 = 3 * m
                if m == 0:
                    for jj in range(4):
                        mj, pj = divmod(jj, 3)
                        load_pair(mp_src[pj][mj], 2 * D, jj % 4, marks=mingru_marks)
                        mingru_lwv[jj] = q_loaded[jj % 4]
                for pi in range(3):
                    j = j0 + pi
                    q = j % 4
                    @block.tensor
                    def _(t):
                        p.w(t, "tensor", mingru_lwv[j][0], mingru_lwv[j][1])
                        p.w(t, "tensor", VE, x1c_ready)
                        bank_wait(t, bs + pi)
                        r = None
                        for k in range(KD):
                            t.matmul(PS[bs + pi][:], slab_hi(q, k), x1hi(k),
                                     start=(k == 0), stop=False)
                            t.matmul(PS[bs + pi][:], slab_hi(q, k), x1lo(k),
                                     start=False, stop=False)
                            r = t.matmul(PS[bs + pi][:], slab_lo(q, k), x1hi(k),
                                         start=False, stop=(k == KD - 1))
                        MM.inc(r, 1)
                    q_free[q] = (MM, MM.n)
                    jn = j + 4
                    if jn < 24:
                        mjn, pjn = divmod(jn, 3)
                        load_pair(mp_src[pjn][mjn], 2 * D, jn % 4,
                                  marks=mingru_marks)
                        mingru_lwv[jn] = q_loaded[jn % 4]
                mm_gvd = MM.n

                @block.scalar
                def _(s):
                    p.w(s, "scalar", MM, mm_gvd)
                    p.w(s, "scalar", LX, aux_loaded)
                    r = s.activation(TMP1[:], PS[bs + 0][:], AF.Sigmoid, bias=vec("bg", m))
                    SE.inc(r, 1)
                    r = s.activation(TMP2[:], PS[bs + 1][:], AF.Tanh, bias=vec("bv", m))
                    SE.inc(r, 1)
                    r = s.activation(TMP3[:], PS[bs + 2][:], AF.Sigmoid, bias=vec("bd", m))
                    SE.inc(r, 1)
                se_gvd = SE.n
                last_gvd_se[0] = se_gvd
                for pi in range(3):
                    p.bank_guard[bs + pi] = (SE, se_gvd)

                @block.vector
                def _(v):
                    p.w(v, "vector", SE, se_gvd)
                    p.w(v, "vector", ST, dump_gate[0])
                    p.w(v, "vector", SE, carry_ready)
                    r = v.tensor_tensor(TMP1[:], TMP1[:], TMP2[:], Op.mult)
                    VE.inc(r, 1)
                    r = v.tensor_scalar(TMP3[:], TMP3[:], 0.998, 0.001, Op.mult, Op.add)
                    VE.inc(r, 1)
                    v.drain()
                    r = v.tensor_tensor_scan(TMP2[:], TMP3[:], TMP1[:],
                                             CAR[:, m:m + 1], Op.mult, Op.add)
                    VE.inc(r, 1)
                    v.drain()
                    r = v.tensor_tensor(xch(m, tt * T4, (tt + 1) * T4),
                                        xch(m, tt * T4, (tt + 1) * T4), TMP2[:], Op.add)
                    VE.inc(r, 1)
                ve_scan = VE.n
                last_scan_ve[0] = ve_scan

                @block.scalar
                def _(s):
                    p.w(s, "scalar", VE, ve_scan)
                    r = s.copy(CAR[:, m:m + 1], TMP2[:, T4 - 1:T4])
                    SE.inc(r, 1)
                    s.drain()
                carry_ready = SE.n
        mingru_done_ve = VE.n
        ws_free[0] = (MM, MM.n)
        ws_free[1] = (MM, MM.n)
        dump_x("s1", mingru_done_ve)

        for tt in range(NT4):
            mmv = rms_pass(tt, 6 + tt % 2, after_ve=mingru_done_ve)
            rstd_ve[tt] = rstd_from_bank(tt, 6 + tt % 2, mmv, vec("eps1", 0), PRB)
        SCRB = SCR.bitcast(mybir.dt.bfloat16)
        x2_prev_hi = {0: 0, 1: 0}
        for tt in range(NT4):
            for k in range(KD):
                TMPa = TMP2 if k % 2 == 0 else TMP3
                @block.vector
                def _(v):
                    p.w(v, "vector", VE, rstd_ve[tt])
                    for semx, valx in dense_last:
                        p.w(v, "vector", semx, valx)
                    p.w(v, "vector", SE, x2_prev_hi[k % 2])
                    r = v.scalar_tensor_tensor(TMPa[:],
                        xch(k, tt * T4, (tt + 1) * T4), vec("wa", k),
                        PRB[:, tt * T4:(tt + 1) * T4], Op.mult, Op.mult)
                    VE.inc(r, 1)
                ve_f = VE.n

                @block.scalar
                def _(s):
                    p.w(s, "scalar", VE, ve_f)
                    r = s.activation(
                        SCRB[:, k * 4096 + tt * T4: k * 4096 + tt * T4 + T4],
                        TMPa[:], AF.Copy, bias=0.0)
                    SE.inc(r, 1)
                hi_se = SE.n
                x2_prev_hi[k % 2] = hi_se

                @block.vector
                def _(v):
                    p.w(v, "vector", SE, hi_se)
                    r = v.tensor_tensor(
                        SCRB[:, k * 4096 + 2048 + tt * T4:
                             k * 4096 + 2048 + tt * T4 + T4],
                        TMPa[:],
                        SCRB[:, k * 4096 + tt * T4: k * 4096 + tt * T4 + T4],
                        Op.subtract)
                    VE.inc(r, 1)
        x2_ready = VE.n

        @block.vector
        def _(v):
            for semx, valx in dense_last:
                p.w(v, "vector", semx, valx)
            v.memset(QKV[:, KOFF:KOFF + 128], 0.0)
            r = v.memset(QKV[:, VOFF:VOFF + 128], 0.0)
            VE.inc(r, 1)
        pads_zeroed = VE.n

        def x2hi(k, tt):
            return SCRB[:, k * 4096 + tt * T4: k * 4096 + tt * T4 + T4]

        def x2lo(k, tt):
            return SCRB[:, k * 4096 + 2048 + tt * T4: k * 4096 + 2048 + tt * T4 + T4]

        pend_wo = [None]
        for i in range(KD):
            dests = [(QOFF, 0), (KOFF + 128, 8), (VOFF + 128, 16)]
            for pi, (doff, moff) in enumerate(dests):
                load_pair(pqkv[moff + i], 2 * D, pi)
                lwv = q_loaded[pi]
                for k in range(KD):
                    @block.tensor
                    def _(t):
                        p.w(t, "tensor", lwv[0], lwv[1])
                        p.w(t, "tensor", VE, x2_ready)
                        r = None
                        for tt in range(NT4):
                            if k == 0:
                                bank_wait(t, tt)
                            t.matmul(PS[tt][:], slab_hi(pi, k), x2hi(k, tt),
                                     start=(k == 0), stop=False)
                            t.matmul(PS[tt][:], slab_hi(pi, k), x2lo(k, tt),
                                     start=False, stop=False)
                            r = t.matmul(PS[tt][:], slab_lo(pi, k), x2hi(k, tt),
                                         start=False, stop=(k == KD - 1))
                        if k == KD - 1:
                            MM.inc(r, 1)
                    q_free[pi] = (MM, MM.n)
                mm_proj = MM.n

                @block.scalar
                def _(s):
                    p.w(s, "scalar", MM, mm_proj)
                    p.w(s, "scalar", ST, dump_gate[0])
                    p.w(s, "scalar", VE, pads_zeroed)
                    for tt in range(NT4):
                        sc = 0.125 if pi == 0 else 1.0
                        r = s.activation(QKV[:, doff + tt * T4: doff + (tt + 1) * T4],
                                         PS[tt][:], AF.Copy, bias=0.0, scale=sc)
                        SE.inc(r, 1)
                se_proj = SE.n
                for tt in range(NT4):
                    p.bank_guard[tt] = (SE, se_proj)
            qkv_se = SE.n
            if i == 0:
                dump_buf("qkv", QKV[:], SE, qkv_se)

            if i == 0:
                load_q(wok[0], 3)
                wo_lwv = q_loaded[3]

            def emit_wo(tt_, av_ve_, lwv_):
                for m in range(KD):
                    bwo = m % 4
                    @block.tensor
                    def _(t):
                        p.w(t, "tensor", lwv_[0], lwv_[1])
                        p.w(t, "tensor", VE, av_ve_)
                        bank_wait(t, bwo)
                        r = t.matmul(PS[bwo][:], slab_q(3, m), AVT[:],
                                     start=True, stop=True)
                        MM.inc(r, 1)
                    mm_wo = MM.n

                    @block.vector
                    def _(v):
                        p.w(v, "vector", MM, mm_wo)
                        p.w(v, "vector", ST, dump_gate[0])
                        r = v.tensor_tensor(xch(m, tt_ * T4, (tt_ + 1) * T4),
                                            PS[bwo][:],
                                            xch(m, tt_ * T4, (tt_ + 1) * T4), Op.add)
                        VE.inc(r, 1)
                    p.bank_guard[bwo] = (VE, VE.n)

            sc_mm_d = {}

            def emit_scores(tt_):
                sc_mm = {}
                for cl in range(4):
                    c = tt_ * 4 + cl
                    for h in range(2):
                        s_ = 2 * cl + h
                        sb_ = 4 + (s_ % 4)
                        scol = (s_ // 4) * 256
                        @block.tensor
                        def _(t):
                            p.w(t, "tensor", SE, qkv_se)
                            p.w(t, "tensor", LX, aux_loaded)
                            bank_wait(t, sb_)
                            q_ap = QKV[h * 64:(h + 1) * 64,
                                       QOFF + c * 128: QOFF + (c + 1) * 128]
                            kk_ap = QKV[h * 64:(h + 1) * 64,
                                        KOFF + c * 128: KOFF + c * 128 + 256]
                            r = t.matmul(PS[sb_][:, scol:scol + 256], q_ap, kk_ap,
                                         start=True, stop=True,
                                         tile_position=(h * 64, 0))
                            MM.inc(r, 1)
                        sc_mm[s_] = MM.n
                sc_mm_d[tt_] = sc_mm

            emit_scores(0)
            prev_trans_mm = 0
            for tt in range(NT4):
                sc_mm = sc_mm_d[tt]

                if pend_wo[0] is not None:
                    emit_wo(*pend_wo[0])
                    pend_wo[0] = None
                    if tt == 0:
                        q_free[3] = (MM, MM.n)
                        load_q(wok[i], 3)
                        wo_lwv = q_loaded[3]

                @block.vector
                def _(v):
                    p.w(v, "vector", ST, dump_gate[0])
                    p.w(v, "vector", MM, prev_trans_mm)
                    for s_ in range(8):
                        c = tt * 4 + s_ // 2
                        sb_ = 4 + (s_ % 4)
                        scol = (s_ // 4) * 256
                        p.w(v, "vector", MM, sc_mm[s_])
                        mk = mask_0 if c == 0 else mask_g
                        r = v.tensor_tensor(PRB[:, s_ * 256:(s_ + 1) * 256],
                                            PS[sb_][:, scol:scol + 256], mk, Op.add)
                        VE.inc(r, 1)
                    v.drain()
                    r = v.tensor_reduce(RWS[:, 0:8].rearrange("p (s o) -> p s o", o=1),
                                        PRB[:].rearrange("p (s k) -> p s k", k=256),
                                        axis=AX.X, op=Op.max, negate=True)
                    VE.inc(r, 1)
                ve_sm1 = VE.n
                for s_ in range(8):
                    p.bank_guard[4 + (s_ % 4)] = (VE, ve_sm1)

                @block.scalar
                def _(s):
                    p.w(s, "scalar", VE, ve_sm1)
                    for s_ in range(8):
                        r = s.activation(PRB[:, s_ * 256:(s_ + 1) * 256],
                                         PRB[:, s_ * 256:(s_ + 1) * 256], AF.Exp,
                                         bias=RWS[:, s_:s_ + 1], scale=1.0,
                                         accum_out=RWS[:, 8 + s_:9 + s_])
                        SE.inc(r, 1)
                se_exp = SE.n

                @block.vector
                def _(v):
                    p.w(v, "vector", SE, se_exp)
                    r = v.reciprocal(RWS[:, 8:16], RWS[:, 8:16])
                    VE.inc(r, 1)
                    v.drain()
                    for s_ in range(8):
                        r = v.tensor_scalar(PRB[:, s_ * 256:(s_ + 1) * 256],
                                            PRB[:, s_ * 256:(s_ + 1) * 256],
                                            RWS[:, 8 + s_:9 + s_], None, Op.mult)
                        VE.inc(r, 1)
                probs_ve = VE.n

                if tt + 1 < NT4:
                    emit_scores(tt + 1)

                for cl in range(4):
                    c = tt * 4 + cl
                    btr = cl % 2
                    bvt = 2 + (cl % 2)
                    ba0 = cl % 2
                    ba1 = 2 + (cl % 2)

                    @block.tensor
                    def _(t):
                        p.w(t, "tensor", VE, probs_ve)
                        bank_wait(t, btr)
                        bank_wait(t, bvt)
                        for h in range(2):
                            for kc in range(2):
                                src2 = PRB[:, (2 * cl + h) * 256 + kc * 128:
                                           (2 * cl + h) * 256 + (kc + 1) * 128]
                                r = t.transpose(PS[btr][:, (2 * h + kc) * 128:
                                                        (2 * h + kc + 1) * 128],
                                                src2, ident)
                                MM.inc(r, 1)
                        vsrc = QKV[:, VOFF + 128 + c * 128: VOFF + 128 + (c + 1) * 128]
                        r = t.transpose(PS[bvt][:, 0:128], vsrc, ident)
                        MM.inc(r, 1)
                    mm_tr = MM.n

                    @block.vector
                    def _(v):
                        p.w(v, "vector", MM, mm_tr)
                        p.w(v, "vector", ST, dump_gate[0])
                        r = v.tensor_copy(PRT[:], PS[btr][:])
                        VE.inc(r, 1)
                        r = v.tensor_copy(VTB[:, (c % 2) * 128:(c % 2 + 1) * 128],
                                          PS[bvt][:, 0:128])
                        VE.inc(r, 1)
                    ve_tr = VE.n
                    p.bank_guard[btr] = (VE, ve_tr)
                    p.bank_guard[bvt] = (VE, ve_tr)

                    @block.tensor
                    def _(t):
                        p.w(t, "tensor", VE, ve_tr)
                        bank_wait(t, ba0)
                        bank_wait(t, ba1)
                        for h, bav in [(0, ba0), (1, ba1)]:
                            for kc in range(2):
                                vt_slot = ((c - 1 + kc) % 2) * 128
                                vt_ap = VTB[:, vt_slot + h * 64: vt_slot + (h + 1) * 64]
                                pt_ap = PRT[:, (2 * h + kc) * 128:(2 * h + kc + 1) * 128]
                                out_ap = PS[bav][h * 64:(h + 1) * 64, 128:256]
                                r = t.matmul(out_ap, vt_ap, pt_ap,
                                             start=(kc == 0), stop=(kc == 1),
                                             tile_position=(0, h * 64))
                                MM.inc(r, 1)
                    mm_av = MM.n

                    @block.vector
                    def _(v):
                        p.w(v, "vector", MM, mm_av)
                        p.w(v, "vector", ST, dump_gate[0])
                        r = v.tensor_copy(AVT[0:64, cl * 128:(cl + 1) * 128],
                                          PS[ba0][0:64, 128:256])
                        VE.inc(r, 1)
                        r = v.tensor_copy(AVT[64:128, cl * 128:(cl + 1) * 128],
                                          PS[ba1][64:128, 128:256])
                        VE.inc(r, 1)
                    p.bank_guard[ba0] = (VE, VE.n)
                    p.bank_guard[ba1] = (VE, VE.n)
                av_ve = VE.n
                prev_trans_mm = MM.n
                if i == 0 and tt == 0:
                    dump_buf("avt", AVT[:], VE, av_ve)
                pend_wo[0] = (tt, av_ve, wo_lwv)
        if pend_wo[0] is not None:
            emit_wo2 = pend_wo[0]
            pend_wo[0] = None
            for m in range(KD):
                bwo = m % 4
                @block.tensor
                def _(t):
                    p.w(t, "tensor", emit_wo2[2][0], emit_wo2[2][1])
                    p.w(t, "tensor", VE, emit_wo2[1])
                    bank_wait(t, bwo)
                    r = t.matmul(PS[bwo][:], slab_q(3, m), AVT[:],
                                 start=True, stop=True)
                    MM.inc(r, 1)
                mm_wo = MM.n

                @block.vector
                def _(v):
                    p.w(v, "vector", MM, mm_wo)
                    p.w(v, "vector", ST, dump_gate[0])
                    r = v.tensor_tensor(xch(m, emit_wo2[0] * T4, (emit_wo2[0] + 1) * T4),
                                        PS[bwo][:],
                                        xch(m, emit_wo2[0] * T4, (emit_wo2[0] + 1) * T4),
                                        Op.add)
                    VE.inc(r, 1)
                p.bank_guard[bwo] = (VE, VE.n)
            q_free[3] = (MM, MM.n)
        attn_done_ve = VE.n
        dump_x("s2", attn_done_ve)

        xr_free = [(None, 0), (None, 0)]

        def layer_norm(tt, with_reread, after_ve, per_m=None):
            if with_reread:
                for m in range(KD):
                    half = m % 2
                    @block.sync
                    def _(e):
                        sem, val = xr_free[half]
                        if sem is not None:
                            p.w(e, "sync", sem, val)
                        r = e.dma_start(out=XR[:, half * 512:(half + 1) * 512],
                                        in_=xT[m][:, tt * T4:(tt + 1) * T4])
                        LRh[half].inc(r, 16)
                    lx_val = LRh[half].n

                    @block.vector
                    def _(v):
                        p.w(v, "vector", LRh[half], lx_val)
                        p.w(v, "vector", ST, dump_gate[0])
                        p.w(v, "vector", VE, after_ve)
                        r = v.tensor_tensor(xch(m, tt * T4, (tt + 1) * T4),
                                            xch(m, tt * T4, (tt + 1) * T4),
                                            XR[:, half * 512:(half + 1) * 512], Op.add)
                        VE.inc(r, 1)
                    xr_free[half] = (VE, VE.n)
            u_ve = VE.n

            for m in range(KD):
                half = m % 2
                @block.scalar
                def _(s):
                    p.w(s, "scalar", VE, max(u_ve, after_ve))
                    p.w(s, "scalar", MM, sq_consumed[half])
                    r = s.activation(XSQ[:, half * 512:(half + 1) * 512],
                                     xch(m, tt * T4, (tt + 1) * T4), AF.Square)
                    SE.inc(r, 1)
                se_val = SE.n

                @block.tensor
                def _(t):
                    p.w(t, "tensor", SE, se_val)
                    p.w(t, "tensor", VE, max(u_ve, after_ve))
                    if m == 0:
                        bank_wait(t, 6)
                        bank_wait(t, 7)
                    r = t.matmul(PS[6][:], ones, xch(m, tt * T4, (tt + 1) * T4),
                                 start=(m == 0), stop=(m == KD - 1))
                    MM.inc(r, 1)
                    r = t.matmul(PS[7][:], ones, XSQ[:, half * 512:(half + 1) * 512],
                                 start=(m == 0), stop=(m == KD - 1))
                    MM.inc(r, 1)
                sq_consumed[half] = MM.n
            mm_stats = MM.n

            @block.vector
            def _(v):
                p.w(v, "vector", MM, mm_stats)
                r = v.tensor_scalar(PRT[:], PS[6][:], 1.0 / D, None, Op.mult)
                VE.inc(r, 1)
                v.drain()
                r = v.tensor_tensor(LNT[:, 0:512], PS[6][:], PRT[:], Op.mult)
                VE.inc(r, 1)
                v.drain()
                r = v.tensor_tensor(LNT[:, 512:1024], PS[7][:], LNT[:, 0:512],
                                    Op.subtract)
                VE.inc(r, 1)
            ve_var = VE.n
            p.bank_guard[6] = (VE, ve_var)
            p.bank_guard[7] = (VE, ve_var)

            @block.scalar
            def _(s):
                p.w(s, "scalar", VE, ve_var)
                r = s.activation(LNT[:, 0:512], LNT[:, 512:1024], AF.Sqrt,
                                 bias=vec("eps2", 0), scale=1.0 / D)
                SE.inc(r, 1)
            se_std = SE.n

            @block.vector
            def _(v):
                p.w(v, "vector", SE, se_std)
                p.w(v, "vector", ST, dump_gate[0])
                r = v.reciprocal(LNT[:, 0:512], LNT[:, 0:512])
                VE.inc(r, 1)
                v.drain()
                for m in range(KD):
                    r = v.tensor_tensor(xch(m, tt * T4, (tt + 1) * T4),
                                        xch(m, tt * T4, (tt + 1) * T4), PRT[:],
                                        Op.subtract)
                    VE.inc(r, 1)
                v.drain()
                for m in range(KD):
                    r = v.tensor_tensor(xch(m, tt * T4, (tt + 1) * T4),
                                        xch(m, tt * T4, (tt + 1) * T4),
                                        LNT[:, 0:512], Op.mult)
                    VE.inc(r, 1)
                    if per_m is not None:
                        per_m[m] = VE.n
            return VE.n


        for tt in range(NT4):
            mmv = rms_pass(tt, 6 + tt % 2, after_ve=attn_done_ve)
            rstd_ve[tt] = rstd_from_bank(tt, 6 + tt % 2, mmv, vec("eps1", 0), PRB)

        QKVB = QKV.bitcast(mybir.dt.bfloat16)
        x3_prev_hi = {0: 0, 1: 0}
        glu_prev_hi = {0: 0, 1: 0}
        x3_ready_d = {}
        x3_consumed = {-1: 0}
        ln1_tt = {}

        def emit_x3(tt_):
            xoff3 = 0
            for k in range(KD):
                TMPa = TMP2 if k % 2 == 0 else TMP3
                @block.vector
                def _(v):
                    p.w(v, "vector", VE, rstd_ve[tt_])
                    p.w(v, "vector", SE, x3_prev_hi[k % 2])
                    p.w(v, "vector", MM, x3_consumed[tt_ - 1])
                    r = v.scalar_tensor_tensor(TMPa[:],
                        xch(k, tt_ * T4, (tt_ + 1) * T4), vec("wf", k),
                        PRB[:, tt_ * T4:(tt_ + 1) * T4], Op.mult, Op.mult)
                    VE.inc(r, 1)
                ve_f = VE.n

                @block.scalar
                def _(s):
                    p.w(s, "scalar", VE, ve_f)
                    r = s.activation(QKVB[:, xoff3 + k * 512: xoff3 + (k + 1) * 512],
                                     TMPa[:], AF.Copy, bias=0.0)
                    SE.inc(r, 1)
                hi_se = SE.n
                x3_prev_hi[k % 2] = hi_se

                @block.vector
                def _(v):
                    p.w(v, "vector", SE, hi_se)
                    r = v.tensor_tensor(
                        QKVB[:, 4096 + xoff3 + k * 512: 4096 + xoff3 + (k + 1) * 512],
                        TMPa[:], QKVB[:, xoff3 + k * 512: xoff3 + (k + 1) * 512],
                        Op.subtract)
                    VE.inc(r, 1)
            x3_ready_d[tt_] = VE.n

        emit_x3(0)
        for tt in range(NT4):
            x3c_ready = x3_ready_d[tt]

            def x3hi(k):
                return QKVB[:, k * 512:(k + 1) * 512]

            def x3lo(k):
                return QKVB[:, 4096 + k * 512: 4096 + (k + 1) * 512]

            for m in range(KF):
                qg = (m % 2) * 2
                qu = qg + 1
                load_pair(pgate[m], 2 * D, qg)
                lw_g = q_loaded[qg]
                load_pair(pup[m], 2 * D, qu)
                lw_u = q_loaded[qu]
                bg_, bu_ = (0, 1) if m % 2 == 0 else (2, 3)
                for lw_, qq, bnk in [(lw_g, qg, bg_), (lw_u, qu, bu_)]:
                    @block.tensor
                    def _(t):
                        p.w(t, "tensor", lw_[0], lw_[1])
                        p.w(t, "tensor", VE, x3c_ready)
                        bank_wait(t, bnk)
                        r = None
                        for k in range(KD):
                            t.matmul(PS[bnk][:], slab_hi(qq, k), x3hi(k),
                                     start=(k == 0), stop=False)
                            t.matmul(PS[bnk][:], slab_hi(qq, k), x3lo(k),
                                     start=False, stop=False)
                            r = t.matmul(PS[bnk][:], slab_lo(qq, k), x3hi(k),
                                         start=False, stop=(k == KD - 1))
                        MM.inc(r, 1)
                    q_free[qq] = (MM, MM.n)
                mm_gu = MM.n

                TMPm = TMP2 if m % 2 == 0 else TMP3
                @block.scalar
                def _(s):
                    p.w(s, "scalar", MM, mm_gu)
                    r = s.activation(TMP1[:], PS[bg_][:], AF.Silu)
                    SE.inc(r, 1)
                se_silu = SE.n
                p.bank_guard[bg_] = (SE, se_silu)

                @block.vector
                def _(v):
                    p.w(v, "vector", SE, se_silu)
                    p.w(v, "vector", SE, glu_prev_hi[m % 2])
                    r = v.tensor_tensor(TMPm[:], PS[bu_][:], TMP1[:], Op.mult)
                    VE.inc(r, 1)
                ve_mid = VE.n
                p.bank_guard[bu_] = (VE, ve_mid)

                @block.scalar
                def _(s):
                    p.w(s, "scalar", VE, ve_mid)
                    r = s.activation(SCRB[:, m * 512:(m + 1) * 512], TMPm[:],
                                     AF.Copy, bias=0.0)
                    SE.inc(r, 1)
                mid_hi_se = SE.n
                glu_prev_hi[m % 2] = mid_hi_se

                @block.vector
                def _(v):
                    p.w(v, "vector", SE, mid_hi_se)
                    r = v.tensor_tensor(
                        SCRB[:, 16384 + m * 512: 16384 + (m + 1) * 512],
                        TMPm[:], SCRB[:, m * 512:(m + 1) * 512], Op.subtract)
                    VE.inc(r, 1)
            mid_ve = VE.n
            x3_consumed[tt] = MM.n
            if tt + 1 < NT4:
                emit_x3(tt + 1)
            del x3hi, x3lo

            for m2 in range(KD):
                bnk = 4 + (m2 % 2)
                lwh = {}
                for kh in range(2):
                    qb = 2 * kh
                    @block.sync
                    def _(e):
                        for qx in (qb, qb + 1):
                            sem, val = q_free[qx]
                            if sem is not None:
                                p.w(e, "sync", sem, val)
                        for semx, valx in dense_last:
                            p.w(e, "sync", semx, valx)
                        r = e.dma_start(out=WSB[:, qb * 2048: qb * 2048 + 4096],
                                        in_=pout[m2][:, kh * 4096:(kh + 1) * 4096])
                        LWh[qb].inc(r, 16)
                    lwh[kh] = (LWh[qb], LWh[qb].n)
                for kh in range(2):
                    qb = 2 * kh
                    @block.tensor
                    def _(t):
                        p.w(t, "tensor", lwh[kh][0], lwh[kh][1])
                        p.w(t, "tensor", VE, mid_ve)
                        if kh == 0:
                            bank_wait(t, bnk)
                        r = None
                        for k in range(16):
                            kk = kh * 16 + k
                            t.matmul(PS[bnk][:], WSB[:, qb * 2048 + k * 128:
                                                     qb * 2048 + (k + 1) * 128],
                                     SCRB[:, kk * 512:(kk + 1) * 512],
                                     start=(kk == 0), stop=False)
                            t.matmul(PS[bnk][:], WSB[:, qb * 2048 + k * 128:
                                                     qb * 2048 + (k + 1) * 128],
                                     SCRB[:, 16384 + kk * 512: 16384 + (kk + 1) * 512],
                                     start=False, stop=False)
                            r = t.matmul(PS[bnk][:], WSB[:, (qb + 1) * 2048 + k * 128:
                                                         (qb + 1) * 2048 + (k + 1) * 128],
                                         SCRB[:, kk * 512:(kk + 1) * 512],
                                         start=False, stop=(kh == 1 and k == 15))
                        MM.inc(r, 1)
                        q_free[qb] = (MM, MM.n)
                        q_free[qb + 1] = (MM, MM.n)
                mm_out = MM.n

                @block.vector
                def _(v):
                    p.w(v, "vector", MM, mm_out)
                    p.w(v, "vector", ST, dump_gate[0])
                    r = v.tensor_tensor(xch(m2, tt * T4, (tt + 1) * T4),
                                        PS[bnk][:],
                                        xch(m2, tt * T4, (tt + 1) * T4), Op.add)
                    VE.inc(r, 1)
                p.bank_guard[bnk] = (VE, VE.n)
            ln1_tt[tt] = layer_norm(tt, True, VE.n)
        glu_done_ve = VE.n
        qkv_free_mm = MM.n
        ws_free[0] = (MM, MM.n)
        ws_free[1] = (MM, MM.n)
        dump_x("s3", glu_done_ve)

        ln1_done = VE.n
        dump_x("vv", ln1_done)

        SCRb = SCR.bitcast(BF16)
        QKVb = QKV.bitcast(BF16)
        A_HI, A_LO, B_HI, B_LO = 0, 8192, 16384, 24576

        w12_mm = {}
        xsplit_ve_d = {}
        last_pair_ve = 0

        def emit_xsplit(t8_):
            tb_ = t8_ * T8
            off = (t8_ % 2) * 4096
            @block.scalar
            def _(s):
                p.w(s, "scalar", MM, w12_mm.get(t8_ - 2, qkv_free_mm))
                p.w(s, "scalar", VE, ln1_tt[t8_ // 2])
                for k in range(KD):
                    r = s.activation(QKVb[:, off + k * T8: off + (k + 1) * T8],
                                     X[:, k * S + tb_: k * S + tb_ + T8],
                                     AF.Copy, bias=0.0)
                    SE.inc(r, 1)
            xs_se = SE.n

            @block.vector
            def _(v):
                p.w(v, "vector", SE, xs_se)
                for k in range(KD):
                    r = v.tensor_tensor(
                        QKVb[:, off + 2048 + k * T8: off + 2048 + (k + 1) * T8],
                        X[:, k * S + tb_: k * S + tb_ + T8],
                        QKVb[:, off + k * T8: off + (k + 1) * T8], Op.subtract)
                    VE.inc(r, 1)
            xsplit_ve_d[t8_] = VE.n

        emit_xsplit(0)
        for t8 in range(NT8):
            tb = t8 * T8
            xsplit_ve = xsplit_ve_d[t8]
            xoff = (t8 % 2) * 4096

            def xhi(k, _o=xoff):
                return QKVb[:, _o + k * T8: _o + (k + 1) * T8]

            def xlo(k, _o=xoff):
                return QKVb[:, _o + 2048 + k * T8: _o + 2048 + (k + 1) * T8]

            for m in range(KF):
                lw_pair = {}
                for pi, wsrc in enumerate([c1, c2]):
                    load_slab8(wsrc[m], D, pi)
                    lw_pair[pi] = ws_loaded[pi]
                b1, b2 = (0, 1) if m % 2 == 0 else (2, 3)
                for pi, bnk in [(0, b1), (1, b2)]:
                    @block.tensor
                    def _(t):
                        p.w(t, "tensor", lw_pair[pi][0], lw_pair[pi][1])
                        p.w(t, "tensor", VE, xsplit_ve)
                        bank_wait(t, bnk)
                        r = None
                        for k in range(KD):
                            t.matmul(PS[bnk][:, 0:T8], slab8(pi, k), xhi(k),
                                     start=(k == 0), stop=False)
                            r = t.matmul(PS[bnk][:, 0:T8], slab8(pi, k), xlo(k),
                                         start=False, stop=(k == KD - 1))
                        MM.inc(r, 1)
                    ws_free[pi] = (MM, MM.n)
                mm12 = MM.n

                TMPa = TMP2 if m % 2 == 0 else TMP3
                @block.scalar
                def _(s):
                    p.w(s, "scalar", MM, mm12)
                    p.w(s, "scalar", VE, last_pair_ve)
                    r = s.activation(TMP1[:, 0:T8], PS[b1][:, 0:T8], AF.Silu,
                                     scale=vec("s1", m))
                    SE.inc(r, 1)
                se_h = SE.n
                p.bank_guard[b1] = (SE, se_h)

                @block.vector
                def _(v):
                    p.w(v, "vector", SE, se_h)
                    r = v.scalar_tensor_tensor(TMPa[:, 0:T8], PS[b2][:, 0:T8],
                                               vec("s2", m), TMP1[:, 0:T8],
                                               Op.mult, Op.mult)
                    VE.inc(r, 1)
                ve_full = VE.n
                p.bank_guard[b2] = (VE, ve_full)

                @block.scalar
                def _(s):
                    p.w(s, "scalar", VE, ve_full)
                    r = s.activation(SCRb[:, A_HI + m * T8: A_HI + (m + 1) * T8],
                                     TMPa[:, 0:T8], AF.Copy, bias=0.0)
                    SE.inc(r, 1)
                hi_se = SE.n

                @block.vector
                def _(v):
                    p.w(v, "vector", SE, hi_se)
                    r = v.tensor_tensor(SCRb[:, A_LO + m * T8: A_LO + (m + 1) * T8],
                                        TMPa[:, 0:T8],
                                        SCRb[:, A_HI + m * T8: A_HI + (m + 1) * T8],
                                        Op.subtract)
                    VE.inc(r, 1)
                last_pair_ve = VE.n
            h0_ve = VE.n
            w12_mm[t8] = MM.n
            if t8 + 1 < NT8:
                emit_xsplit(t8 + 1)

            def h_layer(wsrc, svec, src_hi, src_lo, dst_hi, dst_lo,
                        prev_sem, prev_val):
                nonlocal_last = [last_pair_ve]
                for m in range(KF):
                    bnk = 4 + (m % 2)
                    lwh = {}
                    for kh in range(2):
                        load_slab8(wsrc[m][:, kh * 2048:(kh + 1) * 2048], 2048, kh)
                        lwh[kh] = ws_loaded[kh]
                    for kh in range(2):
                        @block.tensor
                        def _(t):
                            p.w(t, "tensor", lwh[kh][0], lwh[kh][1])
                            p.w(t, "tensor", prev_sem, prev_val)
                            if kh == 0:
                                bank_wait(t, bnk)
                            r = None
                            for k in range(16):
                                kk = kh * 16 + k
                                t.matmul(PS[bnk][:, 0:T8], slab8(kh, k),
                                         SCRb[:, src_hi + kk * T8: src_hi + (kk + 1) * T8],
                                         start=(kk == 0), stop=False)
                                r = t.matmul(PS[bnk][:, 0:T8], slab8(kh, k),
                                             SCRb[:, src_lo + kk * T8: src_lo + (kk + 1) * T8],
                                             start=False, stop=(kk == 31))
                            MM.inc(r, 1)
                            ws_free[kh] = (MM, MM.n)
                    mmh = MM.n

                    TMPa = TMP2 if m % 2 == 0 else TMP3
                    @block.scalar
                    def _(s):
                        p.w(s, "scalar", MM, mmh)
                        p.w(s, "scalar", VE, nonlocal_last[0])
                        r = s.activation(SCRb[:, dst_hi + m * T8: dst_hi + (m + 1) * T8],
                                         PS[bnk][:, 0:T8], AF.Gelu, scale=vec(svec, m))
                        SE.inc(r, 1)
                        r = s.activation(TMPa[:, 0:T8], PS[bnk][:, 0:T8], AF.Gelu,
                                         scale=vec(svec, m))
                        SE.inc(r, 1)
                    se_pair = SE.n
                    p.bank_guard[bnk] = (SE, se_pair)

                    @block.vector
                    def _(v):
                        p.w(v, "vector", SE, se_pair)
                        r = v.tensor_tensor(SCRb[:, dst_lo + m * T8: dst_lo + (m + 1) * T8],
                                            TMPa[:, 0:T8],
                                            SCRb[:, dst_hi + m * T8: dst_hi + (m + 1) * T8],
                                            Op.subtract)
                        VE.inc(r, 1)
                    nonlocal_last[0] = VE.n
                return SE.n, VE.n

            h1_se, h1_ve = h_layer(ch1, "sh1", A_HI, A_LO, B_HI, B_LO, VE, h0_ve)
            h2_se, h2_ve = h_layer(ch2, "sh2", B_HI, B_LO, A_HI, A_LO, VE, h1_ve)

            for m2 in range(KD):
                bnk = 6 + (m2 % 2)
                lwh = {}
                for kh in range(2):
                    load_slab8(c3[m2][:, kh * 2048:(kh + 1) * 2048], 2048, kh)
                    lwh[kh] = ws_loaded[kh]
                for kh in range(2):
                    @block.tensor
                    def _(t):
                        p.w(t, "tensor", lwh[kh][0], lwh[kh][1])
                        p.w(t, "tensor", VE, h2_ve)
                        if kh == 0:
                            bank_wait(t, bnk)
                        r = None
                        for k in range(16):
                            kk = kh * 16 + k
                            t.matmul(PS[bnk][:, 0:T8], slab8(kh, k),
                                     SCRb[:, A_HI + kk * T8: A_HI + (kk + 1) * T8],
                                     start=(kk == 0), stop=False)
                            r = t.matmul(PS[bnk][:, 0:T8], slab8(kh, k),
                                         SCRb[:, A_LO + kk * T8: A_LO + (kk + 1) * T8],
                                         start=False, stop=(kk == 31))
                        MM.inc(r, 1)
                        ws_free[kh] = (MM, MM.n)
                mm3 = MM.n

                @block.vector
                def _(v):
                    p.w(v, "vector", MM, mm3)
                    p.w(v, "vector", ST, dump_gate[0])
                    r = v.scalar_tensor_tensor(X[:, m2 * S + tb: m2 * S + tb + T8], PS[bnk][:, 0:T8], vec("s3", m2), X[:, m2 * S + tb: m2 * S + tb + T8], Op.mult, Op.add)
                    VE.inc(r, 1)
                p.bank_guard[bnk] = (VE, VE.n)
        ffn_done_ve = VE.n
        dump_x("ww", ffn_done_ve)

        out_m = {}
        for tt in range(NT4):
            layer_norm(tt, False, ffn_done_ve,
                       per_m=(out_m if tt == NT4 - 1 else None))
        ln2_done = VE.n

        @block.gpsimd
        def _(g):
            for m in range(KD):
                p.w(g, "gpsimd", VE, out_m[m])
                r = g.dma_start(out=yT[m], in_=xch(m))
                ST.inc(r, 16)
            g.wait_ge(ST.h, ST.n)

    return nc


_NC_CACHE = None


def _get_nc():
    global _NC_CACHE
    if _NC_CACHE is None:
        _NC_CACHE = build_program()
    return _NC_CACHE


def _tile_w(wmat):
    M, K = wmat.shape
    w4 = wmat.reshape(M // 128, 128, K // 128, 128)
    return np.ascontiguousarray(w4.transpose(0, 3, 2, 1)).reshape(M // 128, 128, K)


def _vec_tile(v, n):
    return np.ascontiguousarray(np.asarray(v, np.float32).reshape(n, 128).T)


def kernel(**inp):
    x = np.asarray(inp["L0_attn_x"], dtype=np.float32)
    B = x.shape[0]

    weights = {
        "wg": _tile_w(np.asarray(inp["L0_attn_Wg"], np.float32)),
        "wv": _tile_w(np.asarray(inp["L0_attn_Wv"], np.float32)),
        "wd": _tile_w(np.asarray(inp["L0_attn_Wd"], np.float32)),
        "wqkv": _tile_w(np.asarray(inp["L0_attn_Wqkv"], np.float32)),
        "wo": _tile_w(np.asarray(inp["L0_attn_Wo"], np.float32)),
        "wok": np.ascontiguousarray(
            np.asarray(inp["L0_attn_Wo"], np.float32)
            .reshape(8, 128, 8, 128).transpose(2, 3, 0, 1)).reshape(8, 128, D),
        "wgate": _tile_w(np.asarray(inp["L0_attn_Wgate"], np.float32)),
        "wup": _tile_w(np.asarray(inp["L0_attn_Wup"], np.float32)),
        "wout": _tile_w(np.asarray(inp["L0_attn_Wout"], np.float32)),
        "f1": _tile_w(np.asarray(inp["L0_ffn_w1_w"], np.float32)),
        "f2": _tile_w(np.asarray(inp["L0_ffn_w2_w"], np.float32)),
        "fh1": _tile_w(np.asarray(inp["L0_ffn_h1_w"], np.float32)),
        "fh2": _tile_w(np.asarray(inp["L0_ffn_h2_w"], np.float32)),
        "f3": _tile_w(np.asarray(inp["L0_ffn_w3_w"], np.float32)),
    }

    vecs = np.zeros((128, VECN), np.float32)
    for name, arr in [("bg", inp["L0_attn_bg"]), ("bv", inp["L0_attn_bv"]),
                      ("bd", inp["L0_attn_bd"]), ("wm", inp["L0_attn_w_rms_mix"]),
                      ("wa", inp["L0_attn_w_rms_attn"]), ("wf", inp["L0_attn_w_rms_ffn"]),
                      ("s1", inp["L0_ffn_w1_s"]), ("s2", inp["L0_ffn_w2_s"]),
                      ("sh1", inp["L0_ffn_h1_s"]), ("sh2", inp["L0_ffn_h2_s"]),
                      ("s3", inp["L0_ffn_w3_s"]),
                      ("eps1", np.full(128, RMS_EPS)), ("eps2", np.full(128, LN_EPS))]:
        a, b = VC[name]
        vecs[:, a:b] = _vec_tile(arr, b - a)

    consts = np.zeros((128, CONN), np.float32)
    consts[:, CC["ident"][0]:CC["ident"][1]] = np.eye(128, dtype=np.float32)
    consts[:, CC["ones"][0]:CC["ones"][1]] = 1.0
    j = np.arange(128)[:, None]
    kidx = np.arange(256)[None, :]
    valid = (kidx <= 128 + j) & (kidx >= j + 1)
    consts[:, CC["mask"][0]:CC["mask"][1]] = np.where(valid, 0.0, NEG)
    consts[:, CC["mask0"][0]:CC["mask0"][1]] = np.where(valid & (kidx >= 128), 0.0, NEG)

    in_maps = []
    for b in range(B):
        m = dict(weights)
        m["vecs"] = vecs
        m["consts"] = consts
        m["xT"] = np.ascontiguousarray(x[b].T).reshape(KD, 128, S)
        in_maps.append(m)

    nc = _get_nc()
    res = run_bass_kernel_spmd(nc, in_maps, core_ids=list(range(B)))
    out = np.empty((B, S, D), np.float32)
    for b in range(B):
        out[b] = np.asarray(res.results[b]["yT"]).reshape(D, S).T
    return out


# version 14
# speedup vs baseline: 1.0399x; 1.0399x over previous
import numpy as np
import concourse.bass as bass
import concourse.mybir as mybir
from concourse.bass_utils import run_bass_kernel_spmd
from concourse.alu_op_type import AluOpType as Op

F32 = mybir.dt.float32
AF = mybir.ActivationFunctionType
AX = mybir.AxisListType

D = 1024
S = 2048
DFF = 4096
KD = D // 128
KF = DFF // 128
T4 = 512
NT4 = S // T4
T8 = 256
NT8 = S // T8
RMS_EPS = 1e-6
LN_EPS = 1e-5
NEG = -1.0e30

VC = {}
_c = 0
for _name, _n in [("bg", 8), ("bv", 8), ("bd", 8), ("wm", 8), ("wa", 8), ("wf", 8),
                  ("s1", 32), ("s2", 32), ("sh1", 32), ("sh2", 32), ("s3", 8),
                  ("eps1", 1), ("eps2", 1)]:
    VC[_name] = (_c, _c + _n)
    _c += _n
VECN = _c

CC = {"ident": (0, 128), "ones": (128, 256), "mask": (256, 512), "mask0": (512, 768)}
CONN = 768


class Sem:
    def __init__(self, nc, name):
        self.name = name
        self.h = nc.alloc_semaphore(name)
        self.n = 0

    def inc(self, instr, k=1):
        instr.then_inc(self.h, k)
        self.n += k
        return self.n


class Prog:
    def __init__(self, nc):
        self.nc = nc
        self.sems = {s: Sem(nc, "s" + s) for s in
                     ["LW0", "LW1", "LW2", "LW3", "LX", "LR0", "LR1", "MM", "VE",
                      "SE", "FW", "ST", "CA", "CB", "CD0", "CD1", "CD2", "CD3",
                      "CE", "CI0", "CI1", "CI2", "CI3"]}
        self.last_wait = {}
        self.bank_guard = [None] * 8

    def w(self, eng, eng_name, sem, val):
        if val is None or val <= 0:
            return
        key = (eng_name, sem.name)
        if self.last_wait.get(key, -1) >= val:
            return
        eng.wait_ge(sem.h, val)
        self.last_wait[key] = val


DEBUG_DUMPS = False


def build_program():
    nc = bass.Bass("TRN2", target_bir_lowering=False)

    xT = nc.dram_tensor("xT", [KD, 128, S], F32, kind="ExternalInput").ap()
    yT = nc.dram_tensor("yT", [KD, 128, S], F32, kind="ExternalOutput").ap()
    wg = nc.dram_tensor("wg", [8, 128, D], F32, kind="ExternalInput").ap()
    wv = nc.dram_tensor("wv", [8, 128, D], F32, kind="ExternalInput").ap()
    wd = nc.dram_tensor("wd", [8, 128, D], F32, kind="ExternalInput").ap()
    wqkv = nc.dram_tensor("wqkv", [24, 128, D], F32, kind="ExternalInput").ap()
    wo = nc.dram_tensor("wo", [8, 128, D], F32, kind="ExternalInput").ap()
    wok = nc.dram_tensor("wok", [8, 128, D], F32, kind="ExternalInput").ap()
    wgate = nc.dram_tensor("wgate", [32, 128, D], F32, kind="ExternalInput").ap()
    wup = nc.dram_tensor("wup", [32, 128, D], F32, kind="ExternalInput").ap()
    wout = nc.dram_tensor("wout", [8, 128, DFF], F32, kind="ExternalInput").ap()
    f1 = nc.dram_tensor("f1", [32, 128, D], F32, kind="ExternalInput").ap()
    f2 = nc.dram_tensor("f2", [32, 128, D], F32, kind="ExternalInput").ap()
    fh1 = nc.dram_tensor("fh1", [32, 128, DFF], F32, kind="ExternalInput").ap()
    fh2 = nc.dram_tensor("fh2", [32, 128, DFF], F32, kind="ExternalInput").ap()
    f3 = nc.dram_tensor("f3", [8, 128, DFF], F32, kind="ExternalInput").ap()
    dbg = {}
    if DEBUG_DUMPS:
        for tag in ["s1", "s2", "s3", "vv", "ww", "x2f"]:
            dbg[tag] = nc.dram_tensor("dbg_" + tag, [KD, 128, S], F32,
                                      kind="ExternalOutput").ap()
        dbg["qkv"] = nc.dram_tensor("dbg_qkv", [128, 6400], F32,
                                    kind="ExternalOutput").ap()
        dbg["prb"] = nc.dram_tensor("dbg_prb", [128, 2048], F32,
                                    kind="ExternalOutput").ap()
        dbg["prt"] = nc.dram_tensor("dbg_prt", [128, 512], F32,
                                    kind="ExternalOutput").ap()
        dbg["vtb"] = nc.dram_tensor("dbg_vtb", [128, 256], F32,
                                    kind="ExternalOutput").ap()
        dbg["avt"] = nc.dram_tensor("dbg_avt", [128, 512], F32,
                                    kind="ExternalOutput").ap()
    FP8 = mybir.dt.float8e4
    BF16 = mybir.dt.bfloat16
    c1 = nc.dram_tensor("c1", [32, 128, D], FP8, kind="Internal").ap()
    c2 = nc.dram_tensor("c2", [32, 128, D], FP8, kind="Internal").ap()
    ch1 = nc.dram_tensor("ch1", [32, 128, DFF], FP8, kind="Internal").ap()
    ch2 = nc.dram_tensor("ch2", [32, 128, DFF], FP8, kind="Internal").ap()
    c3 = nc.dram_tensor("c3", [8, 128, DFF], FP8, kind="Internal").ap()
    pqkv = nc.dram_tensor("pqkv", [24, 128, 2 * D], BF16, kind="Internal").ap()
    pgate = nc.dram_tensor("pgate", [32, 128, 2 * D], BF16, kind="Internal").ap()
    pup = nc.dram_tensor("pup", [32, 128, 2 * D], BF16, kind="Internal").ap()
    pout = nc.dram_tensor("pout", [8, 128, 2 * DFF], BF16, kind="Internal").ap()
    vecs = nc.dram_tensor("vecs", [128, VECN], F32, kind="ExternalInput").ap()
    consts = nc.dram_tensor("consts", [128, CONN], F32, kind="ExternalInput").ap()

    X = nc.alloc_sbuf_tensor("X", [128, KD * S], F32).ap()
    SCR = nc.alloc_sbuf_tensor("SCR", [128, 16384], F32).ap()
    QKV = nc.alloc_sbuf_tensor("QKV", [128, 2048 + 2 * 2176], F32).ap()
    PRB = nc.alloc_sbuf_tensor("PRB", [128, 2048], F32).ap()
    PRT = nc.alloc_sbuf_tensor("PRT", [128, 512], F32).ap()
    VTB = nc.alloc_sbuf_tensor("VTB", [128, 256], F32).ap()
    AVT = nc.alloc_sbuf_tensor("AVT", [128, 512], F32).ap()
    XSQ = nc.alloc_sbuf_tensor("XSQ", [128, 2048], F32).ap()
    TMP1 = nc.alloc_sbuf_tensor("TMP1", [128, 512], F32).ap()
    TMP2 = nc.alloc_sbuf_tensor("TMP2", [128, 512], F32).ap()
    TMP3 = nc.alloc_sbuf_tensor("TMP3", [128, 512], F32).ap()
    WS = nc.alloc_sbuf_tensor("WS", [128, 4096], F32).ap()
    XR = nc.alloc_sbuf_tensor("XR", [128, 1024], F32).ap()
    VEC = nc.alloc_sbuf_tensor("VEC", [128, VECN], F32).ap()
    CON = nc.alloc_sbuf_tensor("CON", [128, CONN], F32).ap()
    CAR = nc.alloc_sbuf_tensor("CAR", [128, 8], F32).ap()
    RWS = nc.alloc_sbuf_tensor("RWS", [128, 16], F32).ap()
    LNT = nc.alloc_sbuf_tensor("LNT", [128, 1024], F32).ap()

    PS = [nc.alloc_psum_tensor("psb%d" % i, [128, 512], F32).ap() for i in range(8)]

    QOFF = 0
    KOFF = 2048
    VOFF = 2048 + 2176

    p = Prog(nc)
    (LW0, LW1, LW2, LW3, LX, LR0, LR1, MM, VE, SE, FW, ST, CA, CB, CD0, CD1,
     CD2, CD3, CE, CI0, CI1, CI2, CI3) = (p.sems[k]
        for k in ["LW0", "LW1", "LW2", "LW3", "LX", "LR0", "LR1", "MM", "VE", "SE",
                  "FW", "ST", "CA", "CB", "CD0", "CD1", "CD2", "CD3", "CE", "CI0",
                  "CI1", "CI2", "CI3"])
    CIh = [CI0, CI1, CI2, CI3]
    CDh = [CD0, CD1]
    CD2h = [CD2, CD3]
    LWh = [LW0, LW1, LW2, LW3]
    LRh = [LR0, LR1]

    def vec(name, j):
        a, b = VC[name]
        assert a + j < b
        return VEC[:, a + j:a + j + 1]

    def xch(m, t0=0, t1=S):
        return X[:, m * S + t0: m * S + t1]

    with nc.Block() as block:
        @block.sync
        def _(e):
            r = e.dma_start(out=VEC[:], in_=vecs[:])
            LX.inc(r, 16)
            r = e.dma_start(out=CON[:], in_=consts[:])
            LX.inc(r, 16)
            for m in range(KD):
                r = e.dma_start(out=xch(m), in_=xT[m])
                LX.inc(r, 16)
        aux_loaded = LX.n
        x_loaded = LX.n

        @block.vector
        def _(v):
            r = v.memset(VTB[:], 0.0)
            VE.inc(r, 1)
        pads_zeroed = VE.n

        @block.scalar
        def _(s):
            r = s.memzero(CAR[:])
            SE.inc(r, 1)
        carry_ready = SE.n

        QKVBc = QKV.bitcast(mybir.dt.bfloat16)
        SCRc = SCR.bitcast(mybir.dt.bfloat16)
        cache_marks = []

        @block.gpsimd
        def _(g):
            jobs = []
            for srcw, dstw, nm in [(wqkv, pqkv, 24), (wgate, pgate, 32),
                                   (wup, pup, 32)]:
                for m in range(0, nm, 2):
                    jobs.append((
                        srcw[m:m + 2].rearrange("a p k -> p a k"),
                        True,
                        [(dstw[m], 0, 2048), (dstw[m + 1], 2048, 4096)]))
            for m in range(8):
                for kh in range(2):
                    jobs.append((
                        wout[m][:, kh * 2048:(kh + 1) * 2048],
                        False,
                        [(pout[m][:, kh * 4096:kh * 4096 + 2048], 0, 2048),
                         (pout[m][:, kh * 4096 + 2048:(kh + 1) * 4096],
                          2048, 4096)]))
            cb32 = [QKV[:, 0:2048], QKV[:, 2048:4096]]
            cbhl = [SCRc[:, 24576:28672], SCRc[:, 28672:32768]]
            buf_free = [None, None]
            in_done = []
            cast_done = {}
            def in_ap(b, job):
                if job[1]:
                    return cb32[b][:].rearrange("p (a k) -> p a k", a=2)
                return cb32[b][:]

            def hls(b, job):
                if job[1]:
                    cout = cbhl[b][:].rearrange("p (a two k) -> p a two k",
                                                a=2, two=2)[:, :, 0, :]
                    lout = cbhl[b][:].rearrange("p (a two k) -> p a two k",
                                                a=2, two=2)[:, :, 1, :]
                    cin = cb32[b][:].rearrange("p (a k) -> p a k", a=2)
                else:
                    cout = cbhl[b][:, 0:2048]
                    lout = cbhl[b][:, 2048:4096]
                    cin = cb32[b][:]
                return cout, lout, cin

            def issue_cast(jc):
                bc = jc % 2
                p.w(g, "gpsimd", in_done[jc][0], in_done[jc][1])
                if buf_free[bc] is not None:
                    for sem2, val2 in buf_free[bc]:
                        p.w(g, "gpsimd", sem2, val2)
                cout, lout, cin = hls(bc, jobs[jc])
                r = g.dma_start(out=cout, in_=cin)
                CA.inc(r, 16)
                cast_done[jc] = CA.n

            for j in range(min(2, len(jobs))):
                r = g.dma_start(out=in_ap(j % 2, jobs[j]), in_=jobs[j][0])
                CIh[j % 2].inc(r, 16)
                in_done.append((CIh[j % 2], CIh[j % 2].n))
            issue_cast(0)
            for j, (srcap, inter, outdmas) in enumerate(jobs):
                b = j % 2
                cout, lout, cin = hls(b, jobs[j])
                p.w(g, "gpsimd", CA, cast_done[j])
                r = g.tensor_tensor(lout, cin, cout, Op.subtract)
                CB.inc(r, 1)
                p.w(g, "gpsimd", CB, CB.n)
                mark = []
                for oi, (dst, a0, a1) in enumerate(outdmas):
                    sd = CDh[b] if oi == 0 else CD2h[b]
                    r = g.dma_start(out=dst, in_=cbhl[b][:, a0:a1])
                    sd.inc(r, 16)
                    mark.append((sd, sd.n))
                buf_free[b] = mark
                cache_marks.append(mark)
                if j + 1 < len(jobs):
                    issue_cast(j + 1)
                jn = j + 2
                if jn < len(jobs):
                    r = g.dma_start(out=in_ap(b, jobs[jn]), in_=jobs[jn][0])
                    CIh[jn % 2].inc(r, 16)
                    in_done.append((CIh[jn % 2], CIh[jn % 2].n))
            for srcw, dstw, nm in [(f1, c1, 32), (f2, c2, 32), (fh1, ch1, 32),
                                   (fh2, ch2, 32), (f3, c3, 8)]:
                for m in range(nm):
                    r = g.dma_start(out=dstw[m], in_=srcw[m])
                    CE.inc(r, 16)
        cache_done = CE.n
        dense_last = cache_marks[-1] + cache_marks[-2]

        ident = CON[:, CC["ident"][0]:CC["ident"][1]]
        ones = CON[:, CC["ones"][0]:CC["ones"][1]]
        mask_g = CON[:, CC["mask"][0]:CC["mask"][1]]
        mask_0 = CON[:, CC["mask0"][0]:CC["mask0"][1]]

        ws_free = [(None, 0), (None, 0)]
        ws_loaded = [0, 0]

        def load_slab(src_ap, ncols, half):
            @block.sync
            def _(e):
                sem, val = ws_free[half]
                if sem is not None:
                    p.w(e, "sync", sem, val)
                r = e.dma_start(out=WS[:, half * 2048: half * 2048 + ncols], in_=src_ap)
                LWh[half].inc(r, 16)
            ws_loaded[half] = (LWh[half], LWh[half].n)

        def slab(half, k):
            return WS[:, half * 2048 + k * 128: half * 2048 + (k + 1) * 128]

        q_free = [(None, 0)] * 4
        q_loaded = [0] * 4

        def load_q(src_ap, q):
            @block.sync
            def _(e):
                sem, val = q_free[q]
                if sem is not None:
                    p.w(e, "sync", sem, val)
                r = e.dma_start(out=WS[:, q * 1024:(q + 1) * 1024], in_=src_ap)
                LWh[q].inc(r, 16)
            q_loaded[q] = (LWh[q], LWh[q].n)

        def slab_q(q, k):
            return WS[:, q * 1024 + k * 128: q * 1024 + (k + 1) * 128]

        WS8 = WS.bitcast(mybir.dt.float8e4)

        def load_slab8(src_ap, ncols, half):
            @block.sync
            def _(e):
                sem, val = ws_free[half]
                if sem is not None:
                    p.w(e, "sync", sem, val)
                p.w(e, "sync", CE, cache_done)
                r = e.dma_start(out=WS8[:, half * 8192: half * 8192 + ncols],
                                in_=src_ap)
                LWh[half].inc(r, 16)
            ws_loaded[half] = (LWh[half], LWh[half].n)

        def slab8(half, k):
            return WS8[:, half * 8192 + k * 128: half * 8192 + (k + 1) * 128]

        def bank_wait(t, b):
            g = p.bank_guard[b]
            if g is not None:
                p.w(t, "tensor", g[0], g[1])

        sq_consumed = {0: 0, 1: 0, 2: 0, 3: 0}
        dump_gate = [0]

        def dump_x(tag, ve_val, src_base=None):
            if not DEBUG_DUMPS:
                return
            @block.gpsimd
            def _(g):
                p.w(g, "gpsimd", VE, ve_val)
                for m in range(KD):
                    if src_base is None:
                        r = g.dma_start(out=dbg[tag][m], in_=xch(m))
                    else:
                        r = g.dma_start(out=dbg[tag][m],
                                        in_=SCR[:, m * 2048:(m + 1) * 2048])
                    ST.inc(r, 16)
            dump_gate[0] = ST.n

        def dump_buf(tag, src_ap, sem, val):
            if not DEBUG_DUMPS:
                return
            @block.gpsimd
            def _(g):
                p.w(g, "gpsimd", sem, val)
                r = g.dma_start(out=dbg[tag], in_=src_ap)
                ST.inc(r, 16)
            dump_gate[0] = ST.n

        def rms_pass(tt, bankQ, after_ve=0):
            mmv = 0
            for m in range(KD):
                half = m % 4
                @block.scalar
                def _(s):
                    p.w(s, "scalar", VE, after_ve)
                    p.w(s, "scalar", LX, x_loaded)
                    p.w(s, "scalar", MM, sq_consumed[half])
                    r = s.activation(XSQ[:, half * 512:(half + 1) * 512],
                                     xch(m, tt * T4, (tt + 1) * T4), AF.Square)
                    SE.inc(r, 1)
                se_val = SE.n

                @block.tensor
                def _(t):
                    p.w(t, "tensor", SE, se_val)
                    p.w(t, "tensor", LX, aux_loaded)
                    if m == 0:
                        bank_wait(t, bankQ)
                    r = t.matmul(PS[bankQ][:], ones, XSQ[:, half * 512:(half + 1) * 512],
                                 start=(m == 0), stop=(m == KD - 1))
                    MM.inc(r, 1)
                sq_consumed[half] = MM.n
                mmv = MM.n
            return mmv

        rstd_tmp1_ve = {0: 0, 1: 0}

        def rstd_from_bank(tt, bankQ, mm_val, eps, dest):
            TMPs = TMP1 if tt % 2 == 0 else PRT
            @block.scalar
            def _(s):
                p.w(s, "scalar", MM, mm_val)
                p.w(s, "scalar", VE, rstd_tmp1_ve[tt % 2])
                r = s.activation(TMPs[:], PS[bankQ][:], AF.Sqrt, bias=eps,
                                 scale=1.0 / D)
                SE.inc(r, 1)
            se_val = SE.n
            p.bank_guard[bankQ] = (SE, se_val)

            @block.vector
            def _(v):
                p.w(v, "vector", SE, se_val)
                r = v.reciprocal(dest[:, tt * T4:(tt + 1) * T4], TMPs[:])
                VE.inc(r, 1)
            rstd_tmp1_ve[tt % 2] = VE.n
            return VE.n

        rstd_ve = [0] * NT4
        for tt in range(NT4):
            mmv = rms_pass(tt, 6 + tt % 2)
            rstd_ve[tt] = rstd_from_bank(tt, 6 + tt % 2, mmv, vec("eps1", 0), PRB)

        for tt in range(NT4):
            @block.vector
            def _(v):
                p.w(v, "vector", VE, rstd_ve[tt])
                p.w(v, "vector", LX, aux_loaded)
                for k in range(KD):
                    r = v.scalar_tensor_tensor(SCR[:, k * 512:(k + 1) * 512], xch(k, tt * T4, (tt + 1) * T4), vec("wm", k), PRB[:, tt * T4:(tt + 1) * T4], Op.mult, Op.mult)
                    VE.inc(r, 1)
            x1c_ready = VE.n
            mingru_lwv = {}

            for m in range(KD):
                bs = 0 if m % 2 == 0 else 3
                j0 = 3 * m
                if m == 0:
                    for jj in range(4):
                        mj, pj = divmod(jj, 3)
                        load_q([wg, wv, wd][pj][mj], jj % 4)
                        mingru_lwv[jj] = q_loaded[jj % 4]
                for pi in range(3):
                    j = j0 + pi
                    q = j % 4
                    @block.tensor
                    def _(t):
                        p.w(t, "tensor", mingru_lwv[j][0], mingru_lwv[j][1])
                        p.w(t, "tensor", VE, x1c_ready)
                        bank_wait(t, bs + pi)
                        r = None
                        for k in range(KD):
                            r = t.matmul(PS[bs + pi][:], slab_q(q, k),
                                         SCR[:, k * 512:(k + 1) * 512],
                                         start=(k == 0), stop=(k == KD - 1))
                        MM.inc(r, 1)
                    q_free[q] = (MM, MM.n)
                    jn = j + 4
                    if jn < 24:
                        mjn, pjn = divmod(jn, 3)
                        load_q([wg, wv, wd][pjn][mjn], jn % 4)
                        mingru_lwv[jn] = q_loaded[jn % 4]
                mm_gvd = MM.n

                @block.scalar
                def _(s):
                    p.w(s, "scalar", MM, mm_gvd)
                    p.w(s, "scalar", LX, aux_loaded)
                    r = s.activation(TMP1[:], PS[bs + 0][:], AF.Sigmoid, bias=vec("bg", m))
                    SE.inc(r, 1)
                    r = s.activation(TMP2[:], PS[bs + 1][:], AF.Tanh, bias=vec("bv", m))
                    SE.inc(r, 1)
                    r = s.activation(TMP3[:], PS[bs + 2][:], AF.Sigmoid, bias=vec("bd", m))
                    SE.inc(r, 1)
                se_gvd = SE.n
                for pi in range(3):
                    p.bank_guard[bs + pi] = (SE, se_gvd)

                @block.vector
                def _(v):
                    p.w(v, "vector", SE, se_gvd)
                    p.w(v, "vector", ST, dump_gate[0])
                    p.w(v, "vector", SE, carry_ready)
                    r = v.tensor_tensor(TMP1[:], TMP1[:], TMP2[:], Op.mult)
                    VE.inc(r, 1)
                    r = v.tensor_scalar(TMP3[:], TMP3[:], 0.998, 0.001, Op.mult, Op.add)
                    VE.inc(r, 1)
                    v.drain()
                    r = v.tensor_tensor_scan(TMP2[:], TMP3[:], TMP1[:],
                                             CAR[:, m:m + 1], Op.mult, Op.add)
                    VE.inc(r, 1)
                    v.drain()
                    r = v.tensor_tensor(xch(m, tt * T4, (tt + 1) * T4),
                                        xch(m, tt * T4, (tt + 1) * T4), TMP2[:], Op.add)
                    VE.inc(r, 1)
                ve_scan = VE.n

                @block.scalar
                def _(s):
                    p.w(s, "scalar", VE, ve_scan)
                    r = s.copy(CAR[:, m:m + 1], TMP2[:, T4 - 1:T4])
                    SE.inc(r, 1)
                    s.drain()
                carry_ready = SE.n
        mingru_done_ve = VE.n
        ws_free[0] = (MM, MM.n)
        ws_free[1] = (MM, MM.n)
        dump_x("s1", mingru_done_ve)

        for tt in range(NT4):
            mmv = rms_pass(tt, 6 + tt % 2, after_ve=mingru_done_ve)
            rstd_ve[tt] = rstd_from_bank(tt, 6 + tt % 2, mmv, vec("eps1", 0), PRB)
        SCRB = SCR.bitcast(mybir.dt.bfloat16)
        x2_prev_hi = {0: 0, 1: 0}
        for tt in range(NT4):
            for k in range(KD):
                TMPa = TMP2 if k % 2 == 0 else TMP3
                @block.vector
                def _(v):
                    p.w(v, "vector", VE, rstd_ve[tt])
                    for semx, valx in dense_last:
                        p.w(v, "vector", semx, valx)
                    p.w(v, "vector", SE, x2_prev_hi[k % 2])
                    r = v.scalar_tensor_tensor(TMPa[:],
                        xch(k, tt * T4, (tt + 1) * T4), vec("wa", k),
                        PRB[:, tt * T4:(tt + 1) * T4], Op.mult, Op.mult)
                    VE.inc(r, 1)
                ve_f = VE.n

                @block.scalar
                def _(s):
                    p.w(s, "scalar", VE, ve_f)
                    r = s.activation(
                        SCRB[:, k * 4096 + tt * T4: k * 4096 + tt * T4 + T4],
                        TMPa[:], AF.Copy, bias=0.0)
                    SE.inc(r, 1)
                hi_se = SE.n
                x2_prev_hi[k % 2] = hi_se

                @block.vector
                def _(v):
                    p.w(v, "vector", SE, hi_se)
                    r = v.tensor_tensor(
                        SCRB[:, k * 4096 + 2048 + tt * T4:
                             k * 4096 + 2048 + tt * T4 + T4],
                        TMPa[:],
                        SCRB[:, k * 4096 + tt * T4: k * 4096 + tt * T4 + T4],
                        Op.subtract)
                    VE.inc(r, 1)
        x2_ready = VE.n

        @block.vector
        def _(v):
            for semx, valx in dense_last:
                p.w(v, "vector", semx, valx)
            v.memset(QKV[:, KOFF:KOFF + 128], 0.0)
            r = v.memset(QKV[:, VOFF:VOFF + 128], 0.0)
            VE.inc(r, 1)
        pads_zeroed = VE.n

        def x2hi(k, tt):
            return SCRB[:, k * 4096 + tt * T4: k * 4096 + tt * T4 + T4]

        def x2lo(k, tt):
            return SCRB[:, k * 4096 + 2048 + tt * T4: k * 4096 + 2048 + tt * T4 + T4]

        pend_wo = [None]
        WSB = WS.bitcast(mybir.dt.bfloat16)

        def slab_hi(q, k):
            return WSB[:, q * 2048 + k * 128: q * 2048 + (k + 1) * 128]

        def slab_lo(q, k):
            return WSB[:, q * 2048 + 1024 + k * 128: q * 2048 + 1024 + (k + 1) * 128]

        def load_pair(src_ap, ncols2, q):
            @block.sync
            def _(e):
                sem, val = q_free[q]
                if sem is not None:
                    p.w(e, "sync", sem, val)
                for semx, valx in dense_last:
                    p.w(e, "sync", semx, valx)
                r = e.dma_start(out=WSB[:, q * 2048: q * 2048 + ncols2], in_=src_ap)
                LWh[q].inc(r, 16)
            q_loaded[q] = (LWh[q], LWh[q].n)

        for i in range(KD):
            dests = [(QOFF, 0), (KOFF + 128, 8), (VOFF + 128, 16)]
            for pi, (doff, moff) in enumerate(dests):
                load_pair(pqkv[moff + i], 2 * D, pi)
                lwv = q_loaded[pi]
                for k in range(KD):
                    @block.tensor
                    def _(t):
                        p.w(t, "tensor", lwv[0], lwv[1])
                        p.w(t, "tensor", VE, x2_ready)
                        r = None
                        for tt in range(NT4):
                            if k == 0:
                                bank_wait(t, tt)
                            t.matmul(PS[tt][:], slab_hi(pi, k), x2hi(k, tt),
                                     start=(k == 0), stop=False)
                            t.matmul(PS[tt][:], slab_hi(pi, k), x2lo(k, tt),
                                     start=False, stop=False)
                            r = t.matmul(PS[tt][:], slab_lo(pi, k), x2hi(k, tt),
                                         start=False, stop=(k == KD - 1))
                        if k == KD - 1:
                            MM.inc(r, 1)
                    q_free[pi] = (MM, MM.n)
                mm_proj = MM.n

                @block.scalar
                def _(s):
                    p.w(s, "scalar", MM, mm_proj)
                    p.w(s, "scalar", ST, dump_gate[0])
                    p.w(s, "scalar", VE, pads_zeroed)
                    for tt in range(NT4):
                        sc = 0.125 if pi == 0 else 1.0
                        r = s.activation(QKV[:, doff + tt * T4: doff + (tt + 1) * T4],
                                         PS[tt][:], AF.Copy, bias=0.0, scale=sc)
                        SE.inc(r, 1)
                se_proj = SE.n
                for tt in range(NT4):
                    p.bank_guard[tt] = (SE, se_proj)
            qkv_se = SE.n
            if i == 0:
                dump_buf("qkv", QKV[:], SE, qkv_se)

            if i == 0:
                load_q(wok[0], 3)
                wo_lwv = q_loaded[3]

            def emit_wo(tt_, av_ve_, lwv_):
                for m in range(KD):
                    bwo = m % 4
                    @block.tensor
                    def _(t):
                        p.w(t, "tensor", lwv_[0], lwv_[1])
                        p.w(t, "tensor", VE, av_ve_)
                        bank_wait(t, bwo)
                        r = t.matmul(PS[bwo][:], slab_q(3, m), AVT[:],
                                     start=True, stop=True)
                        MM.inc(r, 1)
                    mm_wo = MM.n

                    @block.vector
                    def _(v):
                        p.w(v, "vector", MM, mm_wo)
                        p.w(v, "vector", ST, dump_gate[0])
                        r = v.tensor_tensor(xch(m, tt_ * T4, (tt_ + 1) * T4),
                                            PS[bwo][:],
                                            xch(m, tt_ * T4, (tt_ + 1) * T4), Op.add)
                        VE.inc(r, 1)
                    p.bank_guard[bwo] = (VE, VE.n)

            sc_mm_d = {}

            def emit_scores(tt_):
                sc_mm = {}
                for cl in range(4):
                    c = tt_ * 4 + cl
                    for h in range(2):
                        s_ = 2 * cl + h
                        sb_ = 4 + (s_ % 4)
                        scol = (s_ // 4) * 256
                        @block.tensor
                        def _(t):
                            p.w(t, "tensor", SE, qkv_se)
                            p.w(t, "tensor", LX, aux_loaded)
                            bank_wait(t, sb_)
                            q_ap = QKV[h * 64:(h + 1) * 64,
                                       QOFF + c * 128: QOFF + (c + 1) * 128]
                            kk_ap = QKV[h * 64:(h + 1) * 64,
                                        KOFF + c * 128: KOFF + c * 128 + 256]
                            r = t.matmul(PS[sb_][:, scol:scol + 256], q_ap, kk_ap,
                                         start=True, stop=True,
                                         tile_position=(h * 64, 0))
                            MM.inc(r, 1)
                        sc_mm[s_] = MM.n
                sc_mm_d[tt_] = sc_mm

            emit_scores(0)
            prev_trans_mm = 0
            for tt in range(NT4):
                sc_mm = sc_mm_d[tt]

                if pend_wo[0] is not None:
                    emit_wo(*pend_wo[0])
                    pend_wo[0] = None
                    if tt == 0:
                        q_free[3] = (MM, MM.n)
                        load_q(wok[i], 3)
                        wo_lwv = q_loaded[3]

                @block.vector
                def _(v):
                    p.w(v, "vector", ST, dump_gate[0])
                    p.w(v, "vector", MM, prev_trans_mm)
                    for s_ in range(8):
                        c = tt * 4 + s_ // 2
                        sb_ = 4 + (s_ % 4)
                        scol = (s_ // 4) * 256
                        p.w(v, "vector", MM, sc_mm[s_])
                        mk = mask_0 if c == 0 else mask_g
                        r = v.tensor_tensor(PRB[:, s_ * 256:(s_ + 1) * 256],
                                            PS[sb_][:, scol:scol + 256], mk, Op.add)
                        VE.inc(r, 1)
                    v.drain()
                    r = v.tensor_reduce(RWS[:, 0:8].rearrange("p (s o) -> p s o", o=1),
                                        PRB[:].rearrange("p (s k) -> p s k", k=256),
                                        axis=AX.X, op=Op.max, negate=True)
                    VE.inc(r, 1)
                ve_sm1 = VE.n
                for s_ in range(8):
                    p.bank_guard[4 + (s_ % 4)] = (VE, ve_sm1)

                @block.scalar
                def _(s):
                    p.w(s, "scalar", VE, ve_sm1)
                    for s_ in range(8):
                        r = s.activation(PRB[:, s_ * 256:(s_ + 1) * 256],
                                         PRB[:, s_ * 256:(s_ + 1) * 256], AF.Exp,
                                         bias=RWS[:, s_:s_ + 1], scale=1.0,
                                         accum_out=RWS[:, 8 + s_:9 + s_])
                        SE.inc(r, 1)
                se_exp = SE.n

                @block.vector
                def _(v):
                    p.w(v, "vector", SE, se_exp)
                    r = v.reciprocal(RWS[:, 8:16], RWS[:, 8:16])
                    VE.inc(r, 1)
                    v.drain()
                    for s_ in range(8):
                        r = v.tensor_scalar(PRB[:, s_ * 256:(s_ + 1) * 256],
                                            PRB[:, s_ * 256:(s_ + 1) * 256],
                                            RWS[:, 8 + s_:9 + s_], None, Op.mult)
                        VE.inc(r, 1)
                probs_ve = VE.n

                if tt + 1 < NT4:
                    emit_scores(tt + 1)

                for cl in range(4):
                    c = tt * 4 + cl
                    btr = cl % 2
                    bvt = 2 + (cl % 2)
                    ba0 = cl % 2
                    ba1 = 2 + (cl % 2)

                    @block.tensor
                    def _(t):
                        p.w(t, "tensor", VE, probs_ve)
                        bank_wait(t, btr)
                        bank_wait(t, bvt)
                        for h in range(2):
                            for kc in range(2):
                                src2 = PRB[:, (2 * cl + h) * 256 + kc * 128:
                                           (2 * cl + h) * 256 + (kc + 1) * 128]
                                r = t.transpose(PS[btr][:, (2 * h + kc) * 128:
                                                        (2 * h + kc + 1) * 128],
                                                src2, ident)
                                MM.inc(r, 1)
                        vsrc = QKV[:, VOFF + 128 + c * 128: VOFF + 128 + (c + 1) * 128]
                        r = t.transpose(PS[bvt][:, 0:128], vsrc, ident)
                        MM.inc(r, 1)
                    mm_tr = MM.n

                    @block.vector
                    def _(v):
                        p.w(v, "vector", MM, mm_tr)
                        p.w(v, "vector", ST, dump_gate[0])
                        r = v.tensor_copy(PRT[:], PS[btr][:])
                        VE.inc(r, 1)
                        r = v.tensor_copy(VTB[:, (c % 2) * 128:(c % 2 + 1) * 128],
                                          PS[bvt][:, 0:128])
                        VE.inc(r, 1)
                    ve_tr = VE.n
                    p.bank_guard[btr] = (VE, ve_tr)
                    p.bank_guard[bvt] = (VE, ve_tr)

                    @block.tensor
                    def _(t):
                        p.w(t, "tensor", VE, ve_tr)
                        bank_wait(t, ba0)
                        bank_wait(t, ba1)
                        for h, bav in [(0, ba0), (1, ba1)]:
                            for kc in range(2):
                                vt_slot = ((c - 1 + kc) % 2) * 128
                                vt_ap = VTB[:, vt_slot + h * 64: vt_slot + (h + 1) * 64]
                                pt_ap = PRT[:, (2 * h + kc) * 128:(2 * h + kc + 1) * 128]
                                out_ap = PS[bav][h * 64:(h + 1) * 64, 128:256]
                                r = t.matmul(out_ap, vt_ap, pt_ap,
                                             start=(kc == 0), stop=(kc == 1),
                                             tile_position=(0, h * 64))
                                MM.inc(r, 1)
                    mm_av = MM.n

                    @block.vector
                    def _(v):
                        p.w(v, "vector", MM, mm_av)
                        p.w(v, "vector", ST, dump_gate[0])
                        r = v.tensor_copy(AVT[0:64, cl * 128:(cl + 1) * 128],
                                          PS[ba0][0:64, 128:256])
                        VE.inc(r, 1)
                        r = v.tensor_copy(AVT[64:128, cl * 128:(cl + 1) * 128],
                                          PS[ba1][64:128, 128:256])
                        VE.inc(r, 1)
                    p.bank_guard[ba0] = (VE, VE.n)
                    p.bank_guard[ba1] = (VE, VE.n)
                av_ve = VE.n
                prev_trans_mm = MM.n
                if i == 0 and tt == 0:
                    dump_buf("avt", AVT[:], VE, av_ve)
                pend_wo[0] = (tt, av_ve, wo_lwv)
        if pend_wo[0] is not None:
            emit_wo2 = pend_wo[0]
            pend_wo[0] = None
            for m in range(KD):
                bwo = m % 4
                @block.tensor
                def _(t):
                    p.w(t, "tensor", emit_wo2[2][0], emit_wo2[2][1])
                    p.w(t, "tensor", VE, emit_wo2[1])
                    bank_wait(t, bwo)
                    r = t.matmul(PS[bwo][:], slab_q(3, m), AVT[:],
                                 start=True, stop=True)
                    MM.inc(r, 1)
                mm_wo = MM.n

                @block.vector
                def _(v):
                    p.w(v, "vector", MM, mm_wo)
                    p.w(v, "vector", ST, dump_gate[0])
                    r = v.tensor_tensor(xch(m, emit_wo2[0] * T4, (emit_wo2[0] + 1) * T4),
                                        PS[bwo][:],
                                        xch(m, emit_wo2[0] * T4, (emit_wo2[0] + 1) * T4),
                                        Op.add)
                    VE.inc(r, 1)
                p.bank_guard[bwo] = (VE, VE.n)
            q_free[3] = (MM, MM.n)
        attn_done_ve = VE.n
        dump_x("s2", attn_done_ve)

        xr_free = [(None, 0), (None, 0)]

        def layer_norm(tt, with_reread, after_ve, per_m=None):
            if with_reread:
                for m in range(KD):
                    half = m % 2
                    @block.sync
                    def _(e):
                        sem, val = xr_free[half]
                        if sem is not None:
                            p.w(e, "sync", sem, val)
                        r = e.dma_start(out=XR[:, half * 512:(half + 1) * 512],
                                        in_=xT[m][:, tt * T4:(tt + 1) * T4])
                        LRh[half].inc(r, 16)
                    lx_val = LRh[half].n

                    @block.vector
                    def _(v):
                        p.w(v, "vector", LRh[half], lx_val)
                        p.w(v, "vector", ST, dump_gate[0])
                        p.w(v, "vector", VE, after_ve)
                        r = v.tensor_tensor(xch(m, tt * T4, (tt + 1) * T4),
                                            xch(m, tt * T4, (tt + 1) * T4),
                                            XR[:, half * 512:(half + 1) * 512], Op.add)
                        VE.inc(r, 1)
                    xr_free[half] = (VE, VE.n)
            u_ve = VE.n

            for m in range(KD):
                half = m % 2
                @block.scalar
                def _(s):
                    p.w(s, "scalar", VE, max(u_ve, after_ve))
                    p.w(s, "scalar", MM, sq_consumed[half])
                    r = s.activation(XSQ[:, half * 512:(half + 1) * 512],
                                     xch(m, tt * T4, (tt + 1) * T4), AF.Square)
                    SE.inc(r, 1)
                se_val = SE.n

                @block.tensor
                def _(t):
                    p.w(t, "tensor", SE, se_val)
                    p.w(t, "tensor", VE, max(u_ve, after_ve))
                    if m == 0:
                        bank_wait(t, 6)
                        bank_wait(t, 7)
                    r = t.matmul(PS[6][:], ones, xch(m, tt * T4, (tt + 1) * T4),
                                 start=(m == 0), stop=(m == KD - 1))
                    MM.inc(r, 1)
                    r = t.matmul(PS[7][:], ones, XSQ[:, half * 512:(half + 1) * 512],
                                 start=(m == 0), stop=(m == KD - 1))
                    MM.inc(r, 1)
                sq_consumed[half] = MM.n
            mm_stats = MM.n

            @block.vector
            def _(v):
                p.w(v, "vector", MM, mm_stats)
                r = v.tensor_scalar(PRT[:], PS[6][:], 1.0 / D, None, Op.mult)
                VE.inc(r, 1)
                v.drain()
                r = v.tensor_tensor(LNT[:, 0:512], PS[6][:], PRT[:], Op.mult)
                VE.inc(r, 1)
                v.drain()
                r = v.tensor_tensor(LNT[:, 512:1024], PS[7][:], LNT[:, 0:512],
                                    Op.subtract)
                VE.inc(r, 1)
            ve_var = VE.n
            p.bank_guard[6] = (VE, ve_var)
            p.bank_guard[7] = (VE, ve_var)

            @block.scalar
            def _(s):
                p.w(s, "scalar", VE, ve_var)
                r = s.activation(LNT[:, 0:512], LNT[:, 512:1024], AF.Sqrt,
                                 bias=vec("eps2", 0), scale=1.0 / D)
                SE.inc(r, 1)
            se_std = SE.n

            @block.vector
            def _(v):
                p.w(v, "vector", SE, se_std)
                p.w(v, "vector", ST, dump_gate[0])
                r = v.reciprocal(LNT[:, 0:512], LNT[:, 0:512])
                VE.inc(r, 1)
                v.drain()
                for m in range(KD):
                    r = v.tensor_tensor(xch(m, tt * T4, (tt + 1) * T4),
                                        xch(m, tt * T4, (tt + 1) * T4), PRT[:],
                                        Op.subtract)
                    VE.inc(r, 1)
                v.drain()
                for m in range(KD):
                    r = v.tensor_tensor(xch(m, tt * T4, (tt + 1) * T4),
                                        xch(m, tt * T4, (tt + 1) * T4),
                                        LNT[:, 0:512], Op.mult)
                    VE.inc(r, 1)
                    if per_m is not None:
                        per_m[m] = VE.n
            return VE.n


        for tt in range(NT4):
            mmv = rms_pass(tt, 6 + tt % 2, after_ve=attn_done_ve)
            rstd_ve[tt] = rstd_from_bank(tt, 6 + tt % 2, mmv, vec("eps1", 0), PRB)

        QKVB = QKV.bitcast(mybir.dt.bfloat16)
        x3_prev_hi = {0: 0, 1: 0}
        glu_prev_hi = {0: 0, 1: 0}
        x3_ready_d = {}
        x3_consumed = {-1: 0}
        ln1_tt = {}

        def emit_x3(tt_):
            xoff3 = 0
            for k in range(KD):
                TMPa = TMP2 if k % 2 == 0 else TMP3
                @block.vector
                def _(v):
                    p.w(v, "vector", VE, rstd_ve[tt_])
                    p.w(v, "vector", SE, x3_prev_hi[k % 2])
                    p.w(v, "vector", MM, x3_consumed[tt_ - 1])
                    r = v.scalar_tensor_tensor(TMPa[:],
                        xch(k, tt_ * T4, (tt_ + 1) * T4), vec("wf", k),
                        PRB[:, tt_ * T4:(tt_ + 1) * T4], Op.mult, Op.mult)
                    VE.inc(r, 1)
                ve_f = VE.n

                @block.scalar
                def _(s):
                    p.w(s, "scalar", VE, ve_f)
                    r = s.activation(QKVB[:, xoff3 + k * 512: xoff3 + (k + 1) * 512],
                                     TMPa[:], AF.Copy, bias=0.0)
                    SE.inc(r, 1)
                hi_se = SE.n
                x3_prev_hi[k % 2] = hi_se

                @block.vector
                def _(v):
                    p.w(v, "vector", SE, hi_se)
                    r = v.tensor_tensor(
                        QKVB[:, 4096 + xoff3 + k * 512: 4096 + xoff3 + (k + 1) * 512],
                        TMPa[:], QKVB[:, xoff3 + k * 512: xoff3 + (k + 1) * 512],
                        Op.subtract)
                    VE.inc(r, 1)
            x3_ready_d[tt_] = VE.n

        emit_x3(0)
        for tt in range(NT4):
            x3c_ready = x3_ready_d[tt]

            def x3hi(k):
                return QKVB[:, k * 512:(k + 1) * 512]

            def x3lo(k):
                return QKVB[:, 4096 + k * 512: 4096 + (k + 1) * 512]

            for m in range(KF):
                qg = (m % 2) * 2
                qu = qg + 1
                load_pair(pgate[m], 2 * D, qg)
                lw_g = q_loaded[qg]
                load_pair(pup[m], 2 * D, qu)
                lw_u = q_loaded[qu]
                bg_, bu_ = (0, 1) if m % 2 == 0 else (2, 3)
                for lw_, qq, bnk in [(lw_g, qg, bg_), (lw_u, qu, bu_)]:
                    @block.tensor
                    def _(t):
                        p.w(t, "tensor", lw_[0], lw_[1])
                        p.w(t, "tensor", VE, x3c_ready)
                        bank_wait(t, bnk)
                        r = None
                        for k in range(KD):
                            t.matmul(PS[bnk][:], slab_hi(qq, k), x3hi(k),
                                     start=(k == 0), stop=False)
                            t.matmul(PS[bnk][:], slab_hi(qq, k), x3lo(k),
                                     start=False, stop=False)
                            r = t.matmul(PS[bnk][:], slab_lo(qq, k), x3hi(k),
                                         start=False, stop=(k == KD - 1))
                        MM.inc(r, 1)
                    q_free[qq] = (MM, MM.n)
                mm_gu = MM.n

                TMPm = TMP2 if m % 2 == 0 else TMP3
                @block.scalar
                def _(s):
                    p.w(s, "scalar", MM, mm_gu)
                    r = s.activation(TMP1[:], PS[bg_][:], AF.Silu)
                    SE.inc(r, 1)
                se_silu = SE.n
                p.bank_guard[bg_] = (SE, se_silu)

                @block.vector
                def _(v):
                    p.w(v, "vector", SE, se_silu)
                    p.w(v, "vector", SE, glu_prev_hi[m % 2])
                    r = v.tensor_tensor(TMPm[:], PS[bu_][:], TMP1[:], Op.mult)
                    VE.inc(r, 1)
                ve_mid = VE.n
                p.bank_guard[bu_] = (VE, ve_mid)

                @block.scalar
                def _(s):
                    p.w(s, "scalar", VE, ve_mid)
                    r = s.activation(SCRB[:, m * 512:(m + 1) * 512], TMPm[:],
                                     AF.Copy, bias=0.0)
                    SE.inc(r, 1)
                mid_hi_se = SE.n
                glu_prev_hi[m % 2] = mid_hi_se

                @block.vector
                def _(v):
                    p.w(v, "vector", SE, mid_hi_se)
                    r = v.tensor_tensor(
                        SCRB[:, 16384 + m * 512: 16384 + (m + 1) * 512],
                        TMPm[:], SCRB[:, m * 512:(m + 1) * 512], Op.subtract)
                    VE.inc(r, 1)
            mid_ve = VE.n
            x3_consumed[tt] = MM.n
            if tt + 1 < NT4:
                emit_x3(tt + 1)
            del x3hi, x3lo

            for m2 in range(KD):
                bnk = 4 + (m2 % 2)
                lwh = {}
                for kh in range(2):
                    qb = 2 * kh
                    @block.sync
                    def _(e):
                        for qx in (qb, qb + 1):
                            sem, val = q_free[qx]
                            if sem is not None:
                                p.w(e, "sync", sem, val)
                        for semx, valx in dense_last:
                            p.w(e, "sync", semx, valx)
                        r = e.dma_start(out=WSB[:, qb * 2048: qb * 2048 + 4096],
                                        in_=pout[m2][:, kh * 4096:(kh + 1) * 4096])
                        LWh[qb].inc(r, 16)
                    lwh[kh] = (LWh[qb], LWh[qb].n)
                for kh in range(2):
                    qb = 2 * kh
                    @block.tensor
                    def _(t):
                        p.w(t, "tensor", lwh[kh][0], lwh[kh][1])
                        p.w(t, "tensor", VE, mid_ve)
                        if kh == 0:
                            bank_wait(t, bnk)
                        r = None
                        for k in range(16):
                            kk = kh * 16 + k
                            t.matmul(PS[bnk][:], WSB[:, qb * 2048 + k * 128:
                                                     qb * 2048 + (k + 1) * 128],
                                     SCRB[:, kk * 512:(kk + 1) * 512],
                                     start=(kk == 0), stop=False)
                            t.matmul(PS[bnk][:], WSB[:, qb * 2048 + k * 128:
                                                     qb * 2048 + (k + 1) * 128],
                                     SCRB[:, 16384 + kk * 512: 16384 + (kk + 1) * 512],
                                     start=False, stop=False)
                            r = t.matmul(PS[bnk][:], WSB[:, (qb + 1) * 2048 + k * 128:
                                                         (qb + 1) * 2048 + (k + 1) * 128],
                                         SCRB[:, kk * 512:(kk + 1) * 512],
                                         start=False, stop=(kh == 1 and k == 15))
                        MM.inc(r, 1)
                        q_free[qb] = (MM, MM.n)
                        q_free[qb + 1] = (MM, MM.n)
                mm_out = MM.n

                @block.vector
                def _(v):
                    p.w(v, "vector", MM, mm_out)
                    p.w(v, "vector", ST, dump_gate[0])
                    r = v.tensor_tensor(xch(m2, tt * T4, (tt + 1) * T4),
                                        PS[bnk][:],
                                        xch(m2, tt * T4, (tt + 1) * T4), Op.add)
                    VE.inc(r, 1)
                p.bank_guard[bnk] = (VE, VE.n)
            ln1_tt[tt] = layer_norm(tt, True, VE.n)
        glu_done_ve = VE.n
        qkv_free_mm = MM.n
        ws_free[0] = (MM, MM.n)
        ws_free[1] = (MM, MM.n)
        dump_x("s3", glu_done_ve)

        ln1_done = VE.n
        dump_x("vv", ln1_done)

        SCRb = SCR.bitcast(BF16)
        QKVb = QKV.bitcast(BF16)
        A_HI, A_LO, B_HI, B_LO = 0, 8192, 16384, 24576

        w12_mm = {}
        xsplit_ve_d = {}
        last_pair_ve = 0

        def emit_xsplit(t8_):
            tb_ = t8_ * T8
            off = (t8_ % 2) * 4096
            @block.scalar
            def _(s):
                p.w(s, "scalar", MM, w12_mm.get(t8_ - 2, qkv_free_mm))
                p.w(s, "scalar", VE, ln1_tt[t8_ // 2])
                for k in range(KD):
                    r = s.activation(QKVb[:, off + k * T8: off + (k + 1) * T8],
                                     X[:, k * S + tb_: k * S + tb_ + T8],
                                     AF.Copy, bias=0.0)
                    SE.inc(r, 1)
            xs_se = SE.n

            @block.vector
            def _(v):
                p.w(v, "vector", SE, xs_se)
                for k in range(KD):
                    r = v.tensor_tensor(
                        QKVb[:, off + 2048 + k * T8: off + 2048 + (k + 1) * T8],
                        X[:, k * S + tb_: k * S + tb_ + T8],
                        QKVb[:, off + k * T8: off + (k + 1) * T8], Op.subtract)
                    VE.inc(r, 1)
            xsplit_ve_d[t8_] = VE.n

        emit_xsplit(0)
        for t8 in range(NT8):
            tb = t8 * T8
            xsplit_ve = xsplit_ve_d[t8]
            xoff = (t8 % 2) * 4096

            def xhi(k, _o=xoff):
                return QKVb[:, _o + k * T8: _o + (k + 1) * T8]

            def xlo(k, _o=xoff):
                return QKVb[:, _o + 2048 + k * T8: _o + 2048 + (k + 1) * T8]

            for m in range(KF):
                lw_pair = {}
                for pi, wsrc in enumerate([c1, c2]):
                    load_slab8(wsrc[m], D, pi)
                    lw_pair[pi] = ws_loaded[pi]
                b1, b2 = (0, 1) if m % 2 == 0 else (2, 3)
                for pi, bnk in [(0, b1), (1, b2)]:
                    @block.tensor
                    def _(t):
                        p.w(t, "tensor", lw_pair[pi][0], lw_pair[pi][1])
                        p.w(t, "tensor", VE, xsplit_ve)
                        bank_wait(t, bnk)
                        r = None
                        for k in range(KD):
                            t.matmul(PS[bnk][:, 0:T8], slab8(pi, k), xhi(k),
                                     start=(k == 0), stop=False)
                            r = t.matmul(PS[bnk][:, 0:T8], slab8(pi, k), xlo(k),
                                         start=False, stop=(k == KD - 1))
                        MM.inc(r, 1)
                    ws_free[pi] = (MM, MM.n)
                mm12 = MM.n

                TMPa = TMP2 if m % 2 == 0 else TMP3
                @block.scalar
                def _(s):
                    p.w(s, "scalar", MM, mm12)
                    p.w(s, "scalar", VE, last_pair_ve)
                    r = s.activation(TMP1[:, 0:T8], PS[b1][:, 0:T8], AF.Silu,
                                     scale=vec("s1", m))
                    SE.inc(r, 1)
                se_h = SE.n
                p.bank_guard[b1] = (SE, se_h)

                @block.vector
                def _(v):
                    p.w(v, "vector", SE, se_h)
                    r = v.scalar_tensor_tensor(TMPa[:, 0:T8], PS[b2][:, 0:T8],
                                               vec("s2", m), TMP1[:, 0:T8],
                                               Op.mult, Op.mult)
                    VE.inc(r, 1)
                ve_full = VE.n
                p.bank_guard[b2] = (VE, ve_full)

                @block.scalar
                def _(s):
                    p.w(s, "scalar", VE, ve_full)
                    r = s.activation(SCRb[:, A_HI + m * T8: A_HI + (m + 1) * T8],
                                     TMPa[:, 0:T8], AF.Copy, bias=0.0)
                    SE.inc(r, 1)
                hi_se = SE.n

                @block.vector
                def _(v):
                    p.w(v, "vector", SE, hi_se)
                    r = v.tensor_tensor(SCRb[:, A_LO + m * T8: A_LO + (m + 1) * T8],
                                        TMPa[:, 0:T8],
                                        SCRb[:, A_HI + m * T8: A_HI + (m + 1) * T8],
                                        Op.subtract)
                    VE.inc(r, 1)
                last_pair_ve = VE.n
            h0_ve = VE.n
            w12_mm[t8] = MM.n
            if t8 + 1 < NT8:
                emit_xsplit(t8 + 1)

            def h_layer(wsrc, svec, src_hi, src_lo, dst_hi, dst_lo,
                        prev_sem, prev_val):
                nonlocal_last = [last_pair_ve]
                for m in range(KF):
                    bnk = 4 + (m % 2)
                    lwh = {}
                    for kh in range(2):
                        load_slab8(wsrc[m][:, kh * 2048:(kh + 1) * 2048], 2048, kh)
                        lwh[kh] = ws_loaded[kh]
                    for kh in range(2):
                        @block.tensor
                        def _(t):
                            p.w(t, "tensor", lwh[kh][0], lwh[kh][1])
                            p.w(t, "tensor", prev_sem, prev_val)
                            if kh == 0:
                                bank_wait(t, bnk)
                            r = None
                            for k in range(16):
                                kk = kh * 16 + k
                                t.matmul(PS[bnk][:, 0:T8], slab8(kh, k),
                                         SCRb[:, src_hi + kk * T8: src_hi + (kk + 1) * T8],
                                         start=(kk == 0), stop=False)
                                r = t.matmul(PS[bnk][:, 0:T8], slab8(kh, k),
                                             SCRb[:, src_lo + kk * T8: src_lo + (kk + 1) * T8],
                                             start=False, stop=(kk == 31))
                            MM.inc(r, 1)
                            ws_free[kh] = (MM, MM.n)
                    mmh = MM.n

                    TMPa = TMP2 if m % 2 == 0 else TMP3
                    @block.scalar
                    def _(s):
                        p.w(s, "scalar", MM, mmh)
                        p.w(s, "scalar", VE, nonlocal_last[0])
                        r = s.activation(SCRb[:, dst_hi + m * T8: dst_hi + (m + 1) * T8],
                                         PS[bnk][:, 0:T8], AF.Gelu, scale=vec(svec, m))
                        SE.inc(r, 1)
                        r = s.activation(TMPa[:, 0:T8], PS[bnk][:, 0:T8], AF.Gelu,
                                         scale=vec(svec, m))
                        SE.inc(r, 1)
                    se_pair = SE.n
                    p.bank_guard[bnk] = (SE, se_pair)

                    @block.vector
                    def _(v):
                        p.w(v, "vector", SE, se_pair)
                        r = v.tensor_tensor(SCRb[:, dst_lo + m * T8: dst_lo + (m + 1) * T8],
                                            TMPa[:, 0:T8],
                                            SCRb[:, dst_hi + m * T8: dst_hi + (m + 1) * T8],
                                            Op.subtract)
                        VE.inc(r, 1)
                    nonlocal_last[0] = VE.n
                return SE.n, VE.n

            h1_se, h1_ve = h_layer(ch1, "sh1", A_HI, A_LO, B_HI, B_LO, VE, h0_ve)
            h2_se, h2_ve = h_layer(ch2, "sh2", B_HI, B_LO, A_HI, A_LO, VE, h1_ve)

            for m2 in range(KD):
                bnk = 6 + (m2 % 2)
                lwh = {}
                for kh in range(2):
                    load_slab8(c3[m2][:, kh * 2048:(kh + 1) * 2048], 2048, kh)
                    lwh[kh] = ws_loaded[kh]
                for kh in range(2):
                    @block.tensor
                    def _(t):
                        p.w(t, "tensor", lwh[kh][0], lwh[kh][1])
                        p.w(t, "tensor", VE, h2_ve)
                        if kh == 0:
                            bank_wait(t, bnk)
                        r = None
                        for k in range(16):
                            kk = kh * 16 + k
                            t.matmul(PS[bnk][:, 0:T8], slab8(kh, k),
                                     SCRb[:, A_HI + kk * T8: A_HI + (kk + 1) * T8],
                                     start=(kk == 0), stop=False)
                            r = t.matmul(PS[bnk][:, 0:T8], slab8(kh, k),
                                         SCRb[:, A_LO + kk * T8: A_LO + (kk + 1) * T8],
                                         start=False, stop=(kk == 31))
                        MM.inc(r, 1)
                        ws_free[kh] = (MM, MM.n)
                mm3 = MM.n

                @block.vector
                def _(v):
                    p.w(v, "vector", MM, mm3)
                    p.w(v, "vector", ST, dump_gate[0])
                    r = v.scalar_tensor_tensor(X[:, m2 * S + tb: m2 * S + tb + T8], PS[bnk][:, 0:T8], vec("s3", m2), X[:, m2 * S + tb: m2 * S + tb + T8], Op.mult, Op.add)
                    VE.inc(r, 1)
                p.bank_guard[bnk] = (VE, VE.n)
        ffn_done_ve = VE.n
        dump_x("ww", ffn_done_ve)

        out_m = {}
        for tt in range(NT4):
            layer_norm(tt, False, ffn_done_ve,
                       per_m=(out_m if tt == NT4 - 1 else None))
        ln2_done = VE.n

        @block.gpsimd
        def _(g):
            for m in range(KD):
                p.w(g, "gpsimd", VE, out_m[m])
                r = g.dma_start(out=yT[m], in_=xch(m))
                ST.inc(r, 16)
            g.wait_ge(ST.h, ST.n)

    return nc


_NC_CACHE = None


def _get_nc():
    global _NC_CACHE
    if _NC_CACHE is None:
        _NC_CACHE = build_program()
    return _NC_CACHE


def _tile_w(wmat):
    M, K = wmat.shape
    w4 = wmat.reshape(M // 128, 128, K // 128, 128)
    return np.ascontiguousarray(w4.transpose(0, 3, 2, 1)).reshape(M // 128, 128, K)


def _vec_tile(v, n):
    return np.ascontiguousarray(np.asarray(v, np.float32).reshape(n, 128).T)


def kernel(**inp):
    x = np.asarray(inp["L0_attn_x"], dtype=np.float32)
    B = x.shape[0]

    weights = {
        "wg": _tile_w(np.asarray(inp["L0_attn_Wg"], np.float32)),
        "wv": _tile_w(np.asarray(inp["L0_attn_Wv"], np.float32)),
        "wd": _tile_w(np.asarray(inp["L0_attn_Wd"], np.float32)),
        "wqkv": _tile_w(np.asarray(inp["L0_attn_Wqkv"], np.float32)),
        "wo": _tile_w(np.asarray(inp["L0_attn_Wo"], np.float32)),
        "wok": np.ascontiguousarray(
            np.asarray(inp["L0_attn_Wo"], np.float32)
            .reshape(8, 128, 8, 128).transpose(2, 3, 0, 1)).reshape(8, 128, D),
        "wgate": _tile_w(np.asarray(inp["L0_attn_Wgate"], np.float32)),
        "wup": _tile_w(np.asarray(inp["L0_attn_Wup"], np.float32)),
        "wout": _tile_w(np.asarray(inp["L0_attn_Wout"], np.float32)),
        "f1": _tile_w(np.asarray(inp["L0_ffn_w1_w"], np.float32)),
        "f2": _tile_w(np.asarray(inp["L0_ffn_w2_w"], np.float32)),
        "fh1": _tile_w(np.asarray(inp["L0_ffn_h1_w"], np.float32)),
        "fh2": _tile_w(np.asarray(inp["L0_ffn_h2_w"], np.float32)),
        "f3": _tile_w(np.asarray(inp["L0_ffn_w3_w"], np.float32)),
    }

    vecs = np.zeros((128, VECN), np.float32)
    for name, arr in [("bg", inp["L0_attn_bg"]), ("bv", inp["L0_attn_bv"]),
                      ("bd", inp["L0_attn_bd"]), ("wm", inp["L0_attn_w_rms_mix"]),
                      ("wa", inp["L0_attn_w_rms_attn"]), ("wf", inp["L0_attn_w_rms_ffn"]),
                      ("s1", inp["L0_ffn_w1_s"]), ("s2", inp["L0_ffn_w2_s"]),
                      ("sh1", inp["L0_ffn_h1_s"]), ("sh2", inp["L0_ffn_h2_s"]),
                      ("s3", inp["L0_ffn_w3_s"]),
                      ("eps1", np.full(128, RMS_EPS)), ("eps2", np.full(128, LN_EPS))]:
        a, b = VC[name]
        vecs[:, a:b] = _vec_tile(arr, b - a)

    consts = np.zeros((128, CONN), np.float32)
    consts[:, CC["ident"][0]:CC["ident"][1]] = np.eye(128, dtype=np.float32)
    consts[:, CC["ones"][0]:CC["ones"][1]] = 1.0
    j = np.arange(128)[:, None]
    kidx = np.arange(256)[None, :]
    valid = (kidx <= 128 + j) & (kidx >= j + 1)
    consts[:, CC["mask"][0]:CC["mask"][1]] = np.where(valid, 0.0, NEG)
    consts[:, CC["mask0"][0]:CC["mask0"][1]] = np.where(valid & (kidx >= 128), 0.0, NEG)

    in_maps = []
    for b in range(B):
        m = dict(weights)
        m["vecs"] = vecs
        m["consts"] = consts
        m["xT"] = np.ascontiguousarray(x[b].T).reshape(KD, 128, S)
        in_maps.append(m)

    nc = _get_nc()
    res = run_bass_kernel_spmd(nc, in_maps, core_ids=list(range(B)))
    out = np.empty((B, S, D), np.float32)
    for b in range(B):
        out[b] = np.asarray(res.results[b]["yT"]).reshape(D, S).T
    return out


# version 15
# speedup vs baseline: 1.0900x; 1.0481x over previous
import numpy as np
import concourse.bass as bass
import concourse.mybir as mybir
from concourse.bass_utils import run_bass_kernel_spmd
from concourse.alu_op_type import AluOpType as Op

F32 = mybir.dt.float32
AF = mybir.ActivationFunctionType
AX = mybir.AxisListType

D = 1024
S = 2048
DFF = 4096
KD = D // 128
KF = DFF // 128
T4 = 512
NT4 = S // T4
T8 = 256
NT8 = S // T8
RMS_EPS = 1e-6
LN_EPS = 1e-5
NEG = -1.0e30

VC = {}
_c = 0
for _name, _n in [("bg", 8), ("bv", 8), ("bd", 8), ("wm", 8), ("wa", 8), ("wf", 8),
                  ("s1", 32), ("s2", 32), ("sh1", 32), ("sh2", 32), ("s3", 8),
                  ("eps1", 1), ("eps2", 1)]:
    VC[_name] = (_c, _c + _n)
    _c += _n
VECN = _c

CC = {"ident": (0, 128), "ones": (128, 256), "mask": (256, 512), "mask0": (512, 768)}
CONN = 768


class Sem:
    def __init__(self, nc, name):
        self.name = name
        self.h = nc.alloc_semaphore(name)
        self.n = 0

    def inc(self, instr, k=1):
        instr.then_inc(self.h, k)
        self.n += k
        return self.n


class Prog:
    def __init__(self, nc):
        self.nc = nc
        self.sems = {s: Sem(nc, "s" + s) for s in
                     ["LW0", "LW1", "LW2", "LW3", "LX", "LR0", "LR1", "MM", "VE",
                      "SE", "FW", "ST", "CA", "CB", "CD0", "CD1", "CD2", "CD3",
                      "CE", "CI0", "CI1", "CI2", "CI3"]}
        self.last_wait = {}
        self.bank_guard = [None] * 8

    def w(self, eng, eng_name, sem, val):
        if val is None or val <= 0:
            return
        key = (eng_name, sem.name)
        if self.last_wait.get(key, -1) >= val:
            return
        eng.wait_ge(sem.h, val)
        self.last_wait[key] = val


DEBUG_DUMPS = False


def build_program():
    nc = bass.Bass("TRN2", target_bir_lowering=False)

    xT = nc.dram_tensor("xT", [KD, 128, S], F32, kind="ExternalInput").ap()
    yT = nc.dram_tensor("yT", [KD, 128, S], F32, kind="ExternalOutput").ap()
    wg = nc.dram_tensor("wg", [8, 128, D], F32, kind="ExternalInput").ap()
    wv = nc.dram_tensor("wv", [8, 128, D], F32, kind="ExternalInput").ap()
    wd = nc.dram_tensor("wd", [8, 128, D], F32, kind="ExternalInput").ap()
    wqkv = nc.dram_tensor("wqkv", [24, 128, D], F32, kind="ExternalInput").ap()
    wo = nc.dram_tensor("wo", [8, 128, D], F32, kind="ExternalInput").ap()
    wok = nc.dram_tensor("wok", [8, 128, D], F32, kind="ExternalInput").ap()
    wgate = nc.dram_tensor("wgate", [32, 128, D], F32, kind="ExternalInput").ap()
    wup = nc.dram_tensor("wup", [32, 128, D], F32, kind="ExternalInput").ap()
    wout = nc.dram_tensor("wout", [8, 128, DFF], F32, kind="ExternalInput").ap()
    f1 = nc.dram_tensor("f1", [32, 128, D], F32, kind="ExternalInput").ap()
    f2 = nc.dram_tensor("f2", [32, 128, D], F32, kind="ExternalInput").ap()
    fh1 = nc.dram_tensor("fh1", [32, 128, DFF], F32, kind="ExternalInput").ap()
    fh2 = nc.dram_tensor("fh2", [32, 128, DFF], F32, kind="ExternalInput").ap()
    f3 = nc.dram_tensor("f3", [8, 128, DFF], F32, kind="ExternalInput").ap()
    dbg = {}
    if DEBUG_DUMPS:
        for tag in ["s1", "s2", "s3", "vv", "ww", "x2f"]:
            dbg[tag] = nc.dram_tensor("dbg_" + tag, [KD, 128, S], F32,
                                      kind="ExternalOutput").ap()
        dbg["qkv"] = nc.dram_tensor("dbg_qkv", [128, 6400], F32,
                                    kind="ExternalOutput").ap()
        dbg["prb"] = nc.dram_tensor("dbg_prb", [128, 2048], F32,
                                    kind="ExternalOutput").ap()
        dbg["prt"] = nc.dram_tensor("dbg_prt", [128, 512], F32,
                                    kind="ExternalOutput").ap()
        dbg["vtb"] = nc.dram_tensor("dbg_vtb", [128, 256], F32,
                                    kind="ExternalOutput").ap()
        dbg["avt"] = nc.dram_tensor("dbg_avt", [128, 512], F32,
                                    kind="ExternalOutput").ap()
    FP8 = mybir.dt.float8e4
    BF16 = mybir.dt.bfloat16
    c1 = nc.dram_tensor("c1", [32, 128, D], FP8, kind="Internal").ap()
    c2 = nc.dram_tensor("c2", [32, 128, D], FP8, kind="Internal").ap()
    ch1 = nc.dram_tensor("ch1", [32, 128, DFF], FP8, kind="Internal").ap()
    ch2 = nc.dram_tensor("ch2", [32, 128, DFF], FP8, kind="Internal").ap()
    c3 = nc.dram_tensor("c3", [8, 128, DFF], FP8, kind="Internal").ap()
    pqkv = nc.dram_tensor("pqkv", [24, 128, 2 * D], BF16, kind="Internal").ap()
    pgate = nc.dram_tensor("pgate", [32, 128, 2 * D], BF16, kind="Internal").ap()
    pup = nc.dram_tensor("pup", [32, 128, 2 * D], BF16, kind="Internal").ap()
    pout = nc.dram_tensor("pout", [8, 128, 2 * DFF], BF16, kind="Internal").ap()
    vecs = nc.dram_tensor("vecs", [128, VECN], F32, kind="ExternalInput").ap()
    consts = nc.dram_tensor("consts", [128, CONN], F32, kind="ExternalInput").ap()

    X = nc.alloc_sbuf_tensor("X", [128, KD * S], F32).ap()
    SCR = nc.alloc_sbuf_tensor("SCR", [128, 16384], F32).ap()
    QKV = nc.alloc_sbuf_tensor("QKV", [128, 2048 + 2 * 2176], F32).ap()
    PRB = nc.alloc_sbuf_tensor("PRB", [128, 2048], F32).ap()
    PRT = nc.alloc_sbuf_tensor("PRT", [128, 512], F32).ap()
    VTB = nc.alloc_sbuf_tensor("VTB", [128, 256], F32).ap()
    AVT = nc.alloc_sbuf_tensor("AVT", [128, 512], F32).ap()
    XSQ = nc.alloc_sbuf_tensor("XSQ", [128, 2048], F32).ap()
    TMP1 = nc.alloc_sbuf_tensor("TMP1", [128, 512], F32).ap()
    TMP2 = nc.alloc_sbuf_tensor("TMP2", [128, 512], F32).ap()
    TMP3 = nc.alloc_sbuf_tensor("TMP3", [128, 512], F32).ap()
    WS = nc.alloc_sbuf_tensor("WS", [128, 4096], F32).ap()
    XR = nc.alloc_sbuf_tensor("XR", [128, 1024], F32).ap()
    VEC = nc.alloc_sbuf_tensor("VEC", [128, VECN], F32).ap()
    CON = nc.alloc_sbuf_tensor("CON", [128, CONN], F32).ap()
    CAR = nc.alloc_sbuf_tensor("CAR", [128, 8], F32).ap()
    RWS = nc.alloc_sbuf_tensor("RWS", [128, 16], F32).ap()
    LNT = nc.alloc_sbuf_tensor("LNT", [128, 1024], F32).ap()

    PS = [nc.alloc_psum_tensor("psb%d" % i, [128, 512], F32).ap() for i in range(8)]

    QOFF = 0
    KOFF = 2048
    VOFF = 2048 + 2176

    p = Prog(nc)
    (LW0, LW1, LW2, LW3, LX, LR0, LR1, MM, VE, SE, FW, ST, CA, CB, CD0, CD1,
     CD2, CD3, CE, CI0, CI1, CI2, CI3) = (p.sems[k]
        for k in ["LW0", "LW1", "LW2", "LW3", "LX", "LR0", "LR1", "MM", "VE", "SE",
                  "FW", "ST", "CA", "CB", "CD0", "CD1", "CD2", "CD3", "CE", "CI0",
                  "CI1", "CI2", "CI3"])
    CIh = [CI0, CI1, CI2, CI3]
    CDh = [CD0, CD1]
    CD2h = [CD2, CD3]
    LWh = [LW0, LW1, LW2, LW3]
    LRh = [LR0, LR1]

    def vec(name, j):
        a, b = VC[name]
        assert a + j < b
        return VEC[:, a + j:a + j + 1]

    def xch(m, t0=0, t1=S):
        return X[:, m * S + t0: m * S + t1]

    with nc.Block() as block:
        @block.sync
        def _(e):
            r = e.dma_start(out=VEC[:], in_=vecs[:])
            LX.inc(r, 16)
            r = e.dma_start(out=CON[:], in_=consts[:])
            LX.inc(r, 16)
            for m in range(KD):
                r = e.dma_start(out=xch(m), in_=xT[m])
                LX.inc(r, 16)
        aux_loaded = LX.n
        x_loaded = LX.n

        @block.vector
        def _(v):
            r = v.memset(VTB[:], 0.0)
            VE.inc(r, 1)
        pads_zeroed = VE.n

        @block.scalar
        def _(s):
            r = s.memzero(CAR[:])
            SE.inc(r, 1)
        carry_ready = SE.n

        QKVBc = QKV.bitcast(mybir.dt.bfloat16)
        SCRc = SCR.bitcast(mybir.dt.bfloat16)
        cache_marks = []

        @block.gpsimd
        def _(g):
            jobs = []
            for srcw, dstw, nm in [(wqkv, pqkv, 24), (wgate, pgate, 32),
                                   (wup, pup, 32)]:
                for m in range(0, nm, 2):
                    jobs.append((
                        srcw[m:m + 2].rearrange("a p k -> p a k"),
                        True,
                        [(dstw[m], 0, 2048), (dstw[m + 1], 2048, 4096)]))
            for m in range(8):
                for kh in range(2):
                    jobs.append((
                        wout[m][:, kh * 2048:(kh + 1) * 2048],
                        False,
                        [(pout[m][:, kh * 4096:kh * 4096 + 2048], 0, 2048),
                         (pout[m][:, kh * 4096 + 2048:(kh + 1) * 4096],
                          2048, 4096)]))
            cb32 = [QKV[:, 0:2048], QKV[:, 2048:4096]]
            cbhl = [SCRc[:, 24576:28672], SCRc[:, 28672:32768]]
            buf_free = [None, None]
            in_done = []
            cast_done = {}
            def in_ap(b, job):
                if job[1]:
                    return cb32[b][:].rearrange("p (a k) -> p a k", a=2)
                return cb32[b][:]

            def hls(b, job):
                if job[1]:
                    cout = cbhl[b][:].rearrange("p (a two k) -> p a two k",
                                                a=2, two=2)[:, :, 0, :]
                    lout = cbhl[b][:].rearrange("p (a two k) -> p a two k",
                                                a=2, two=2)[:, :, 1, :]
                    cin = cb32[b][:].rearrange("p (a k) -> p a k", a=2)
                else:
                    cout = cbhl[b][:, 0:2048]
                    lout = cbhl[b][:, 2048:4096]
                    cin = cb32[b][:]
                return cout, lout, cin

            def issue_cast(jc):
                bc = jc % 2
                p.w(g, "gpsimd", in_done[jc][0], in_done[jc][1])
                if buf_free[bc] is not None:
                    for sem2, val2 in buf_free[bc]:
                        p.w(g, "gpsimd", sem2, val2)
                cout, lout, cin = hls(bc, jobs[jc])
                r = g.dma_start(out=cout, in_=cin)
                CA.inc(r, 16)
                cast_done[jc] = CA.n

            for j in range(min(2, len(jobs))):
                r = g.dma_start(out=in_ap(j % 2, jobs[j]), in_=jobs[j][0])
                CIh[j % 2].inc(r, 16)
                in_done.append((CIh[j % 2], CIh[j % 2].n))
            issue_cast(0)
            for j, (srcap, inter, outdmas) in enumerate(jobs):
                b = j % 2
                cout, lout, cin = hls(b, jobs[j])
                p.w(g, "gpsimd", CA, cast_done[j])
                r = g.tensor_tensor(lout, cin, cout, Op.subtract)
                CB.inc(r, 1)
                p.w(g, "gpsimd", CB, CB.n)
                mark = []
                for oi, (dst, a0, a1) in enumerate(outdmas):
                    sd = CDh[b] if oi == 0 else CD2h[b]
                    r = g.dma_start(out=dst, in_=cbhl[b][:, a0:a1])
                    sd.inc(r, 16)
                    mark.append((sd, sd.n))
                buf_free[b] = mark
                cache_marks.append(mark)
                if j + 1 < len(jobs):
                    issue_cast(j + 1)
                jn = j + 2
                if jn < len(jobs):
                    r = g.dma_start(out=in_ap(b, jobs[jn]), in_=jobs[jn][0])
                    CIh[jn % 2].inc(r, 16)
                    in_done.append((CIh[jn % 2], CIh[jn % 2].n))
            for srcw, dstw, nm in [(f1, c1, 32), (f2, c2, 32), (fh1, ch1, 32),
                                   (fh2, ch2, 32), (f3, c3, 8)]:
                for m in range(nm):
                    r = g.dma_start(out=dstw[m], in_=srcw[m])
                    CE.inc(r, 16)
        cache_done = CE.n
        dense_last = cache_marks[-1] + cache_marks[-2]

        ident = CON[:, CC["ident"][0]:CC["ident"][1]]
        ones = CON[:, CC["ones"][0]:CC["ones"][1]]
        mask_g = CON[:, CC["mask"][0]:CC["mask"][1]]
        mask_0 = CON[:, CC["mask0"][0]:CC["mask0"][1]]

        ws_free = [(None, 0), (None, 0)]
        ws_loaded = [0, 0]

        def load_slab(src_ap, ncols, half):
            @block.sync
            def _(e):
                sem, val = ws_free[half]
                if sem is not None:
                    p.w(e, "sync", sem, val)
                r = e.dma_start(out=WS[:, half * 2048: half * 2048 + ncols], in_=src_ap)
                LWh[half].inc(r, 16)
            ws_loaded[half] = (LWh[half], LWh[half].n)

        def slab(half, k):
            return WS[:, half * 2048 + k * 128: half * 2048 + (k + 1) * 128]

        q_free = [(None, 0)] * 4
        q_loaded = [0] * 4

        def load_q(src_ap, q):
            @block.sync
            def _(e):
                sem, val = q_free[q]
                if sem is not None:
                    p.w(e, "sync", sem, val)
                r = e.dma_start(out=WS[:, q * 1024:(q + 1) * 1024], in_=src_ap)
                LWh[q].inc(r, 16)
            q_loaded[q] = (LWh[q], LWh[q].n)

        def slab_q(q, k):
            return WS[:, q * 1024 + k * 128: q * 1024 + (k + 1) * 128]

        WS8 = WS.bitcast(mybir.dt.float8e4)

        def load_slab8(src_ap, ncols, half):
            @block.sync
            def _(e):
                sem, val = ws_free[half]
                if sem is not None:
                    p.w(e, "sync", sem, val)
                p.w(e, "sync", CE, cache_done)
                r = e.dma_start(out=WS8[:, half * 8192: half * 8192 + ncols],
                                in_=src_ap)
                LWh[half].inc(r, 16)
            ws_loaded[half] = (LWh[half], LWh[half].n)

        def slab8(half, k):
            return WS8[:, half * 8192 + k * 128: half * 8192 + (k + 1) * 128]

        def bank_wait(t, b):
            g = p.bank_guard[b]
            if g is not None:
                p.w(t, "tensor", g[0], g[1])

        sq_consumed = {0: 0, 1: 0, 2: 0, 3: 0}
        dump_gate = [0]

        def dump_x(tag, ve_val, src_base=None):
            if not DEBUG_DUMPS:
                return
            @block.gpsimd
            def _(g):
                p.w(g, "gpsimd", VE, ve_val)
                for m in range(KD):
                    if src_base is None:
                        r = g.dma_start(out=dbg[tag][m], in_=xch(m))
                    else:
                        r = g.dma_start(out=dbg[tag][m],
                                        in_=SCR[:, m * 2048:(m + 1) * 2048])
                    ST.inc(r, 16)
            dump_gate[0] = ST.n

        def dump_buf(tag, src_ap, sem, val):
            if not DEBUG_DUMPS:
                return
            @block.gpsimd
            def _(g):
                p.w(g, "gpsimd", sem, val)
                r = g.dma_start(out=dbg[tag], in_=src_ap)
                ST.inc(r, 16)
            dump_gate[0] = ST.n

        def rms_pass(tt, bankQ, after_ve=0):
            mmv = 0
            for m in range(KD):
                half = m % 4
                @block.scalar
                def _(s):
                    p.w(s, "scalar", VE, after_ve)
                    p.w(s, "scalar", LX, x_loaded)
                    p.w(s, "scalar", MM, sq_consumed[half])
                    r = s.activation(XSQ[:, half * 512:(half + 1) * 512],
                                     xch(m, tt * T4, (tt + 1) * T4), AF.Square)
                    SE.inc(r, 1)
                se_val = SE.n

                @block.tensor
                def _(t):
                    p.w(t, "tensor", SE, se_val)
                    p.w(t, "tensor", LX, aux_loaded)
                    if m == 0:
                        bank_wait(t, bankQ)
                    r = t.matmul(PS[bankQ][:], ones, XSQ[:, half * 512:(half + 1) * 512],
                                 start=(m == 0), stop=(m == KD - 1))
                    MM.inc(r, 1)
                sq_consumed[half] = MM.n
                mmv = MM.n
            return mmv

        rstd_tmp1_ve = {0: 0, 1: 0}

        def rstd_from_bank(tt, bankQ, mm_val, eps, dest):
            TMPs = TMP1 if tt % 2 == 0 else PRT
            @block.scalar
            def _(s):
                p.w(s, "scalar", MM, mm_val)
                p.w(s, "scalar", VE, rstd_tmp1_ve[tt % 2])
                r = s.activation(TMPs[:], PS[bankQ][:], AF.Sqrt, bias=eps,
                                 scale=1.0 / D)
                SE.inc(r, 1)
            se_val = SE.n
            p.bank_guard[bankQ] = (SE, se_val)

            @block.vector
            def _(v):
                p.w(v, "vector", SE, se_val)
                r = v.reciprocal(dest[:, tt * T4:(tt + 1) * T4], TMPs[:])
                VE.inc(r, 1)
            rstd_tmp1_ve[tt % 2] = VE.n
            return VE.n

        rstd_ve = [0] * NT4
        for tt in range(NT4):
            mmv = rms_pass(tt, 6 + tt % 2)
            rstd_ve[tt] = rstd_from_bank(tt, 6 + tt % 2, mmv, vec("eps1", 0), PRB)

        for tt in range(NT4):
            @block.vector
            def _(v):
                p.w(v, "vector", VE, rstd_ve[tt])
                p.w(v, "vector", LX, aux_loaded)
                for k in range(KD):
                    r = v.scalar_tensor_tensor(SCR[:, k * 512:(k + 1) * 512], xch(k, tt * T4, (tt + 1) * T4), vec("wm", k), PRB[:, tt * T4:(tt + 1) * T4], Op.mult, Op.mult)
                    VE.inc(r, 1)
            x1c_ready = VE.n
            mingru_lwv = {}

            for m in range(KD):
                bs = 0 if m % 2 == 0 else 3
                j0 = 3 * m
                if m == 0:
                    for jj in range(4):
                        mj, pj = divmod(jj, 3)
                        load_q([wg, wv, wd][pj][mj], jj % 4)
                        mingru_lwv[jj] = q_loaded[jj % 4]
                for pi in range(3):
                    j = j0 + pi
                    q = j % 4
                    @block.tensor
                    def _(t):
                        p.w(t, "tensor", mingru_lwv[j][0], mingru_lwv[j][1])
                        p.w(t, "tensor", VE, x1c_ready)
                        bank_wait(t, bs + pi)
                        r = None
                        for k in range(KD):
                            r = t.matmul(PS[bs + pi][:], slab_q(q, k),
                                         SCR[:, k * 512:(k + 1) * 512],
                                         start=(k == 0), stop=(k == KD - 1))
                        MM.inc(r, 1)
                    q_free[q] = (MM, MM.n)
                    jn = j + 4
                    if jn < 24:
                        mjn, pjn = divmod(jn, 3)
                        load_q([wg, wv, wd][pjn][mjn], jn % 4)
                        mingru_lwv[jn] = q_loaded[jn % 4]
                mm_gvd = MM.n

                @block.scalar
                def _(s):
                    p.w(s, "scalar", MM, mm_gvd)
                    p.w(s, "scalar", LX, aux_loaded)
                    r = s.activation(TMP1[:], PS[bs + 0][:], AF.Sigmoid, bias=vec("bg", m))
                    SE.inc(r, 1)
                    r = s.activation(TMP2[:], PS[bs + 1][:], AF.Tanh, bias=vec("bv", m))
                    SE.inc(r, 1)
                    r = s.activation(TMP3[:], PS[bs + 2][:], AF.Sigmoid, bias=vec("bd", m))
                    SE.inc(r, 1)
                se_gvd = SE.n
                for pi in range(3):
                    p.bank_guard[bs + pi] = (SE, se_gvd)

                @block.vector
                def _(v):
                    p.w(v, "vector", SE, se_gvd)
                    p.w(v, "vector", ST, dump_gate[0])
                    p.w(v, "vector", SE, carry_ready)
                    r = v.tensor_tensor(TMP1[:], TMP1[:], TMP2[:], Op.mult)
                    VE.inc(r, 1)
                    r = v.tensor_scalar(TMP3[:], TMP3[:], 0.998, 0.001, Op.mult, Op.add)
                    VE.inc(r, 1)
                    v.drain()
                    r = v.tensor_tensor_scan(TMP2[:], TMP3[:], TMP1[:],
                                             CAR[:, m:m + 1], Op.mult, Op.add)
                    VE.inc(r, 1)
                    v.drain()
                    r = v.tensor_tensor(xch(m, tt * T4, (tt + 1) * T4),
                                        xch(m, tt * T4, (tt + 1) * T4), TMP2[:], Op.add)
                    VE.inc(r, 1)
                ve_scan = VE.n

                @block.scalar
                def _(s):
                    p.w(s, "scalar", VE, ve_scan)
                    r = s.copy(CAR[:, m:m + 1], TMP2[:, T4 - 1:T4])
                    SE.inc(r, 1)
                    s.drain()
                carry_ready = SE.n
        mingru_done_ve = VE.n
        ws_free[0] = (MM, MM.n)
        ws_free[1] = (MM, MM.n)
        dump_x("s1", mingru_done_ve)

        for tt in range(NT4):
            mmv = rms_pass(tt, 6 + tt % 2, after_ve=mingru_done_ve)
            rstd_ve[tt] = rstd_from_bank(tt, 6 + tt % 2, mmv, vec("eps1", 0), PRB)
        SCRB = SCR.bitcast(mybir.dt.bfloat16)
        x2_prev_hi = {0: 0, 1: 0}
        for tt in range(NT4):
            for k in range(KD):
                TMPa = TMP2 if k % 2 == 0 else TMP3
                @block.vector
                def _(v):
                    p.w(v, "vector", VE, rstd_ve[tt])
                    for semx, valx in dense_last:
                        p.w(v, "vector", semx, valx)
                    p.w(v, "vector", SE, x2_prev_hi[k % 2])
                    r = v.scalar_tensor_tensor(TMPa[:],
                        xch(k, tt * T4, (tt + 1) * T4), vec("wa", k),
                        PRB[:, tt * T4:(tt + 1) * T4], Op.mult, Op.mult)
                    VE.inc(r, 1)
                ve_f = VE.n

                @block.scalar
                def _(s):
                    p.w(s, "scalar", VE, ve_f)
                    r = s.activation(
                        SCRB[:, k * 4096 + tt * T4: k * 4096 + tt * T4 + T4],
                        TMPa[:], AF.Copy, bias=0.0)
                    SE.inc(r, 1)
                hi_se = SE.n
                x2_prev_hi[k % 2] = hi_se

                @block.vector
                def _(v):
                    p.w(v, "vector", SE, hi_se)
                    r = v.tensor_tensor(
                        SCRB[:, k * 4096 + 2048 + tt * T4:
                             k * 4096 + 2048 + tt * T4 + T4],
                        TMPa[:],
                        SCRB[:, k * 4096 + tt * T4: k * 4096 + tt * T4 + T4],
                        Op.subtract)
                    VE.inc(r, 1)
        x2_ready = VE.n

        @block.vector
        def _(v):
            for semx, valx in dense_last:
                p.w(v, "vector", semx, valx)
            v.memset(QKV[:, KOFF:KOFF + 128], 0.0)
            r = v.memset(QKV[:, VOFF:VOFF + 128], 0.0)
            VE.inc(r, 1)
        pads_zeroed = VE.n

        def x2hi(k, tt):
            return SCRB[:, k * 4096 + tt * T4: k * 4096 + tt * T4 + T4]

        def x2lo(k, tt):
            return SCRB[:, k * 4096 + 2048 + tt * T4: k * 4096 + 2048 + tt * T4 + T4]

        pend_wo = [None]
        WSB = WS.bitcast(mybir.dt.bfloat16)

        def slab_hi(q, k):
            return WSB[:, q * 2048 + k * 128: q * 2048 + (k + 1) * 128]

        def slab_lo(q, k):
            return WSB[:, q * 2048 + 1024 + k * 128: q * 2048 + 1024 + (k + 1) * 128]

        def load_pair(src_ap, ncols2, q):
            @block.sync
            def _(e):
                sem, val = q_free[q]
                if sem is not None:
                    p.w(e, "sync", sem, val)
                for semx, valx in dense_last:
                    p.w(e, "sync", semx, valx)
                r = e.dma_start(out=WSB[:, q * 2048: q * 2048 + ncols2], in_=src_ap)
                LWh[q].inc(r, 16)
            q_loaded[q] = (LWh[q], LWh[q].n)

        for i in range(KD):
            dests = [(QOFF, 0), (KOFF + 128, 8), (VOFF + 128, 16)]
            for pi, (doff, moff) in enumerate(dests):
                load_pair(pqkv[moff + i], 2 * D, pi)
                lwv = q_loaded[pi]
                for k in range(KD):
                    @block.tensor
                    def _(t):
                        p.w(t, "tensor", lwv[0], lwv[1])
                        p.w(t, "tensor", VE, x2_ready)
                        r = None
                        for tt in range(NT4):
                            if k == 0:
                                bank_wait(t, tt)
                            t.matmul(PS[tt][:], slab_hi(pi, k), x2hi(k, tt),
                                     start=(k == 0), stop=False)
                            t.matmul(PS[tt][:], slab_hi(pi, k), x2lo(k, tt),
                                     start=False, stop=False)
                            r = t.matmul(PS[tt][:], slab_lo(pi, k), x2hi(k, tt),
                                         start=False, stop=(k == KD - 1))
                        if k == KD - 1:
                            MM.inc(r, 1)
                    q_free[pi] = (MM, MM.n)
                mm_proj = MM.n

                @block.scalar
                def _(s):
                    p.w(s, "scalar", MM, mm_proj)
                    p.w(s, "scalar", ST, dump_gate[0])
                    p.w(s, "scalar", VE, pads_zeroed)
                    for tt in range(NT4):
                        sc = 0.125 if pi == 0 else 1.0
                        r = s.activation(QKV[:, doff + tt * T4: doff + (tt + 1) * T4],
                                         PS[tt][:], AF.Copy, bias=0.0, scale=sc)
                        SE.inc(r, 1)
                se_proj = SE.n
                for tt in range(NT4):
                    p.bank_guard[tt] = (SE, se_proj)
            qkv_se = SE.n
            if i == 0:
                dump_buf("qkv", QKV[:], SE, qkv_se)

            if i == 0:
                load_q(wok[0], 3)
                wo_lwv = q_loaded[3]

            def emit_wo(tt_, av_ve_, lwv_):
                for m in range(KD):
                    bwo = m % 4
                    @block.tensor
                    def _(t):
                        p.w(t, "tensor", lwv_[0], lwv_[1])
                        p.w(t, "tensor", VE, av_ve_)
                        bank_wait(t, bwo)
                        r = t.matmul(PS[bwo][:], slab_q(3, m), AVT[:],
                                     start=True, stop=True)
                        MM.inc(r, 1)
                    mm_wo = MM.n

                    @block.vector
                    def _(v):
                        p.w(v, "vector", MM, mm_wo)
                        p.w(v, "vector", ST, dump_gate[0])
                        r = v.tensor_tensor(xch(m, tt_ * T4, (tt_ + 1) * T4),
                                            PS[bwo][:],
                                            xch(m, tt_ * T4, (tt_ + 1) * T4), Op.add)
                        VE.inc(r, 1)
                    p.bank_guard[bwo] = (VE, VE.n)

            sc_mm_d = {}

            def emit_scores(tt_):
                sc_mm = {}
                for cl in range(4):
                    c = tt_ * 4 + cl
                    for h in range(2):
                        s_ = 2 * cl + h
                        sb_ = 4 + (s_ % 4)
                        scol = (s_ // 4) * 256
                        @block.tensor
                        def _(t):
                            p.w(t, "tensor", SE, qkv_se)
                            p.w(t, "tensor", LX, aux_loaded)
                            bank_wait(t, sb_)
                            q_ap = QKV[h * 64:(h + 1) * 64,
                                       QOFF + c * 128: QOFF + (c + 1) * 128]
                            kk_ap = QKV[h * 64:(h + 1) * 64,
                                        KOFF + c * 128: KOFF + c * 128 + 256]
                            r = t.matmul(PS[sb_][:, scol:scol + 256], q_ap, kk_ap,
                                         start=True, stop=True,
                                         tile_position=(h * 64, 0))
                            MM.inc(r, 1)
                        sc_mm[s_] = MM.n
                sc_mm_d[tt_] = sc_mm

            emit_scores(0)
            prev_trans_mm = 0
            for tt in range(NT4):
                sc_mm = sc_mm_d[tt]

                if pend_wo[0] is not None:
                    emit_wo(*pend_wo[0])
                    pend_wo[0] = None
                    if tt == 0:
                        q_free[3] = (MM, MM.n)
                        load_q(wok[i], 3)
                        wo_lwv = q_loaded[3]

                @block.vector
                def _(v):
                    p.w(v, "vector", ST, dump_gate[0])
                    p.w(v, "vector", MM, prev_trans_mm)
                    for s_ in range(8):
                        c = tt * 4 + s_ // 2
                        sb_ = 4 + (s_ % 4)
                        scol = (s_ // 4) * 256
                        p.w(v, "vector", MM, sc_mm[s_])
                        mk = mask_0 if c == 0 else mask_g
                        r = v.tensor_tensor(PRB[:, s_ * 256:(s_ + 1) * 256],
                                            PS[sb_][:, scol:scol + 256], mk, Op.add)
                        VE.inc(r, 1)
                    v.drain()
                    r = v.tensor_reduce(RWS[:, 0:8].rearrange("p (s o) -> p s o", o=1),
                                        PRB[:].rearrange("p (s k) -> p s k", k=256),
                                        axis=AX.X, op=Op.max, negate=True)
                    VE.inc(r, 1)
                ve_sm1 = VE.n
                for s_ in range(8):
                    p.bank_guard[4 + (s_ % 4)] = (VE, ve_sm1)

                @block.scalar
                def _(s):
                    p.w(s, "scalar", VE, ve_sm1)
                    for s_ in range(8):
                        r = s.activation(PRB[:, s_ * 256:(s_ + 1) * 256],
                                         PRB[:, s_ * 256:(s_ + 1) * 256], AF.Exp,
                                         bias=RWS[:, s_:s_ + 1], scale=1.0,
                                         accum_out=RWS[:, 8 + s_:9 + s_])
                        SE.inc(r, 1)
                se_exp = SE.n

                @block.vector
                def _(v):
                    p.w(v, "vector", SE, se_exp)
                    r = v.reciprocal(RWS[:, 8:16], RWS[:, 8:16])
                    VE.inc(r, 1)
                    v.drain()
                    for s_ in range(8):
                        r = v.tensor_scalar(PRB[:, s_ * 256:(s_ + 1) * 256],
                                            PRB[:, s_ * 256:(s_ + 1) * 256],
                                            RWS[:, 8 + s_:9 + s_], None, Op.mult)
                        VE.inc(r, 1)
                probs_ve = VE.n

                if tt + 1 < NT4:
                    emit_scores(tt + 1)

                for cl in range(4):
                    c = tt * 4 + cl
                    btr = cl % 2
                    bvt = 2 + (cl % 2)
                    ba0 = cl % 2
                    ba1 = 2 + (cl % 2)

                    @block.tensor
                    def _(t):
                        p.w(t, "tensor", VE, probs_ve)
                        bank_wait(t, btr)
                        bank_wait(t, bvt)
                        for h in range(2):
                            for kc in range(2):
                                src2 = PRB[:, (2 * cl + h) * 256 + kc * 128:
                                           (2 * cl + h) * 256 + (kc + 1) * 128]
                                r = t.transpose(PS[btr][:, (2 * h + kc) * 128:
                                                        (2 * h + kc + 1) * 128],
                                                src2, ident)
                                MM.inc(r, 1)
                        vsrc = QKV[:, VOFF + 128 + c * 128: VOFF + 128 + (c + 1) * 128]
                        r = t.transpose(PS[bvt][:, 0:128], vsrc, ident)
                        MM.inc(r, 1)
                    mm_tr = MM.n

                    @block.vector
                    def _(v):
                        p.w(v, "vector", MM, mm_tr)
                        p.w(v, "vector", ST, dump_gate[0])
                        r = v.tensor_copy(PRT[:], PS[btr][:])
                        VE.inc(r, 1)
                        r = v.tensor_copy(VTB[:, (c % 2) * 128:(c % 2 + 1) * 128],
                                          PS[bvt][:, 0:128])
                        VE.inc(r, 1)
                    ve_tr = VE.n
                    p.bank_guard[btr] = (VE, ve_tr)
                    p.bank_guard[bvt] = (VE, ve_tr)

                    @block.tensor
                    def _(t):
                        p.w(t, "tensor", VE, ve_tr)
                        bank_wait(t, ba0)
                        bank_wait(t, ba1)
                        for h, bav in [(0, ba0), (1, ba1)]:
                            for kc in range(2):
                                vt_slot = ((c - 1 + kc) % 2) * 128
                                vt_ap = VTB[:, vt_slot + h * 64: vt_slot + (h + 1) * 64]
                                pt_ap = PRT[:, (2 * h + kc) * 128:(2 * h + kc + 1) * 128]
                                out_ap = PS[bav][h * 64:(h + 1) * 64, 128:256]
                                r = t.matmul(out_ap, vt_ap, pt_ap,
                                             start=(kc == 0), stop=(kc == 1),
                                             tile_position=(0, h * 64))
                                MM.inc(r, 1)
                    mm_av = MM.n

                    @block.vector
                    def _(v):
                        p.w(v, "vector", MM, mm_av)
                        p.w(v, "vector", ST, dump_gate[0])
                        r = v.tensor_copy(AVT[0:64, cl * 128:(cl + 1) * 128],
                                          PS[ba0][0:64, 128:256])
                        VE.inc(r, 1)
                        r = v.tensor_copy(AVT[64:128, cl * 128:(cl + 1) * 128],
                                          PS[ba1][64:128, 128:256])
                        VE.inc(r, 1)
                    p.bank_guard[ba0] = (VE, VE.n)
                    p.bank_guard[ba1] = (VE, VE.n)
                av_ve = VE.n
                prev_trans_mm = MM.n
                if i == 0 and tt == 0:
                    dump_buf("avt", AVT[:], VE, av_ve)
                pend_wo[0] = (tt, av_ve, wo_lwv)
        if pend_wo[0] is not None:
            emit_wo2 = pend_wo[0]
            pend_wo[0] = None
            for m in range(KD):
                bwo = m % 4
                @block.tensor
                def _(t):
                    p.w(t, "tensor", emit_wo2[2][0], emit_wo2[2][1])
                    p.w(t, "tensor", VE, emit_wo2[1])
                    bank_wait(t, bwo)
                    r = t.matmul(PS[bwo][:], slab_q(3, m), AVT[:],
                                 start=True, stop=True)
                    MM.inc(r, 1)
                mm_wo = MM.n

                @block.vector
                def _(v):
                    p.w(v, "vector", MM, mm_wo)
                    p.w(v, "vector", ST, dump_gate[0])
                    r = v.tensor_tensor(xch(m, emit_wo2[0] * T4, (emit_wo2[0] + 1) * T4),
                                        PS[bwo][:],
                                        xch(m, emit_wo2[0] * T4, (emit_wo2[0] + 1) * T4),
                                        Op.add)
                    VE.inc(r, 1)
                p.bank_guard[bwo] = (VE, VE.n)
            q_free[3] = (MM, MM.n)
        attn_done_ve = VE.n
        dump_x("s2", attn_done_ve)

        xr_free = [(None, 0), (None, 0)]

        def layer_norm(tt, with_reread, after_ve, per_m=None):
            if with_reread:
                for m in range(KD):
                    half = m % 2
                    @block.sync
                    def _(e):
                        sem, val = xr_free[half]
                        if sem is not None:
                            p.w(e, "sync", sem, val)
                        r = e.dma_start(out=XR[:, half * 512:(half + 1) * 512],
                                        in_=xT[m][:, tt * T4:(tt + 1) * T4])
                        LRh[half].inc(r, 16)
                    lx_val = LRh[half].n

                    @block.vector
                    def _(v):
                        p.w(v, "vector", LRh[half], lx_val)
                        p.w(v, "vector", ST, dump_gate[0])
                        p.w(v, "vector", VE, after_ve)
                        r = v.tensor_tensor(xch(m, tt * T4, (tt + 1) * T4),
                                            xch(m, tt * T4, (tt + 1) * T4),
                                            XR[:, half * 512:(half + 1) * 512], Op.add)
                        VE.inc(r, 1)
                    xr_free[half] = (VE, VE.n)
            u_ve = VE.n

            for m in range(KD):
                half = m % 2
                @block.scalar
                def _(s):
                    p.w(s, "scalar", VE, max(u_ve, after_ve))
                    p.w(s, "scalar", MM, sq_consumed[half])
                    r = s.activation(XSQ[:, half * 512:(half + 1) * 512],
                                     xch(m, tt * T4, (tt + 1) * T4), AF.Square)
                    SE.inc(r, 1)
                se_val = SE.n

                @block.tensor
                def _(t):
                    p.w(t, "tensor", SE, se_val)
                    p.w(t, "tensor", VE, max(u_ve, after_ve))
                    if m == 0:
                        bank_wait(t, 6)
                        bank_wait(t, 7)
                    r = t.matmul(PS[6][:], ones, xch(m, tt * T4, (tt + 1) * T4),
                                 start=(m == 0), stop=(m == KD - 1))
                    MM.inc(r, 1)
                    r = t.matmul(PS[7][:], ones, XSQ[:, half * 512:(half + 1) * 512],
                                 start=(m == 0), stop=(m == KD - 1))
                    MM.inc(r, 1)
                sq_consumed[half] = MM.n
            mm_stats = MM.n

            @block.vector
            def _(v):
                p.w(v, "vector", MM, mm_stats)
                r = v.tensor_scalar(PRT[:], PS[6][:], 1.0 / D, None, Op.mult)
                VE.inc(r, 1)
                v.drain()
                r = v.tensor_tensor(LNT[:, 0:512], PS[6][:], PRT[:], Op.mult)
                VE.inc(r, 1)
                v.drain()
                r = v.tensor_tensor(LNT[:, 512:1024], PS[7][:], LNT[:, 0:512],
                                    Op.subtract)
                VE.inc(r, 1)
            ve_var = VE.n
            p.bank_guard[6] = (VE, ve_var)
            p.bank_guard[7] = (VE, ve_var)

            @block.scalar
            def _(s):
                p.w(s, "scalar", VE, ve_var)
                r = s.activation(LNT[:, 0:512], LNT[:, 512:1024], AF.Sqrt,
                                 bias=vec("eps2", 0), scale=1.0 / D)
                SE.inc(r, 1)
            se_std = SE.n

            @block.vector
            def _(v):
                p.w(v, "vector", SE, se_std)
                p.w(v, "vector", ST, dump_gate[0])
                r = v.reciprocal(LNT[:, 0:512], LNT[:, 0:512])
                VE.inc(r, 1)
                v.drain()
                for m in range(KD):
                    r = v.tensor_tensor(xch(m, tt * T4, (tt + 1) * T4),
                                        xch(m, tt * T4, (tt + 1) * T4), PRT[:],
                                        Op.subtract)
                    VE.inc(r, 1)
                v.drain()
                for m in range(KD):
                    r = v.tensor_tensor(xch(m, tt * T4, (tt + 1) * T4),
                                        xch(m, tt * T4, (tt + 1) * T4),
                                        LNT[:, 0:512], Op.mult)
                    VE.inc(r, 1)
                    if per_m is not None:
                        per_m[m] = VE.n
            return VE.n


        for tt in range(NT4):
            mmv = rms_pass(tt, 6 + tt % 2, after_ve=attn_done_ve)
            rstd_ve[tt] = rstd_from_bank(tt, 6 + tt % 2, mmv, vec("eps1", 0), PRB)

        QKVB = QKV.bitcast(mybir.dt.bfloat16)
        x3_prev_hi = {0: 0, 1: 0}
        glu_prev_hi = {0: 0, 1: 0}
        x3_ready_d = {}
        x3_consumed = {-1: 0}
        ln1_tt = {}
        wout_ve_d = {}

        def emit_x3(tt_):
            xoff3 = 0
            for k in range(KD):
                TMPa = TMP2 if k % 2 == 0 else TMP3
                @block.vector
                def _(v):
                    p.w(v, "vector", VE, rstd_ve[tt_])
                    p.w(v, "vector", SE, x3_prev_hi[k % 2])
                    p.w(v, "vector", MM, x3_consumed[tt_ - 1])
                    r = v.scalar_tensor_tensor(TMPa[:],
                        xch(k, tt_ * T4, (tt_ + 1) * T4), vec("wf", k),
                        PRB[:, tt_ * T4:(tt_ + 1) * T4], Op.mult, Op.mult)
                    VE.inc(r, 1)
                ve_f = VE.n

                @block.scalar
                def _(s):
                    p.w(s, "scalar", VE, ve_f)
                    r = s.activation(QKVB[:, xoff3 + k * 512: xoff3 + (k + 1) * 512],
                                     TMPa[:], AF.Copy, bias=0.0)
                    SE.inc(r, 1)
                hi_se = SE.n
                x3_prev_hi[k % 2] = hi_se

                @block.vector
                def _(v):
                    p.w(v, "vector", SE, hi_se)
                    r = v.tensor_tensor(
                        QKVB[:, 4096 + xoff3 + k * 512: 4096 + xoff3 + (k + 1) * 512],
                        TMPa[:], QKVB[:, xoff3 + k * 512: xoff3 + (k + 1) * 512],
                        Op.subtract)
                    VE.inc(r, 1)
            x3_ready_d[tt_] = VE.n

        emit_x3(0)
        for tt in range(NT4):
            x3c_ready = x3_ready_d[tt]

            def x3hi(k):
                return QKVB[:, k * 512:(k + 1) * 512]

            def x3lo(k):
                return QKVB[:, 4096 + k * 512: 4096 + (k + 1) * 512]

            for m in range(KF):
                if m == 2 and tt > 0:
                    ln1_tt[tt - 1] = layer_norm(tt - 1, True, wout_ve_d[tt - 1])
                qg = (m % 2) * 2
                qu = qg + 1
                load_pair(pgate[m], 2 * D, qg)
                lw_g = q_loaded[qg]
                load_pair(pup[m], 2 * D, qu)
                lw_u = q_loaded[qu]
                bg_, bu_ = (0, 1) if m % 2 == 0 else (2, 3)
                for lw_, qq, bnk in [(lw_g, qg, bg_), (lw_u, qu, bu_)]:
                    @block.tensor
                    def _(t):
                        p.w(t, "tensor", lw_[0], lw_[1])
                        p.w(t, "tensor", VE, x3c_ready)
                        bank_wait(t, bnk)
                        r = None
                        for k in range(KD):
                            t.matmul(PS[bnk][:], slab_hi(qq, k), x3hi(k),
                                     start=(k == 0), stop=False)
                            t.matmul(PS[bnk][:], slab_hi(qq, k), x3lo(k),
                                     start=False, stop=False)
                            r = t.matmul(PS[bnk][:], slab_lo(qq, k), x3hi(k),
                                         start=False, stop=(k == KD - 1))
                        MM.inc(r, 1)
                    q_free[qq] = (MM, MM.n)
                mm_gu = MM.n

                TMPm = TMP2 if m % 2 == 0 else TMP3
                @block.scalar
                def _(s):
                    p.w(s, "scalar", MM, mm_gu)
                    r = s.activation(TMP1[:], PS[bg_][:], AF.Silu)
                    SE.inc(r, 1)
                se_silu = SE.n
                p.bank_guard[bg_] = (SE, se_silu)

                @block.vector
                def _(v):
                    p.w(v, "vector", SE, se_silu)
                    p.w(v, "vector", SE, glu_prev_hi[m % 2])
                    r = v.tensor_tensor(TMPm[:], PS[bu_][:], TMP1[:], Op.mult)
                    VE.inc(r, 1)
                ve_mid = VE.n
                p.bank_guard[bu_] = (VE, ve_mid)

                @block.scalar
                def _(s):
                    p.w(s, "scalar", VE, ve_mid)
                    r = s.activation(SCRB[:, m * 512:(m + 1) * 512], TMPm[:],
                                     AF.Copy, bias=0.0)
                    SE.inc(r, 1)
                mid_hi_se = SE.n
                glu_prev_hi[m % 2] = mid_hi_se

                @block.vector
                def _(v):
                    p.w(v, "vector", SE, mid_hi_se)
                    r = v.tensor_tensor(
                        SCRB[:, 16384 + m * 512: 16384 + (m + 1) * 512],
                        TMPm[:], SCRB[:, m * 512:(m + 1) * 512], Op.subtract)
                    VE.inc(r, 1)
            mid_ve = VE.n
            x3_consumed[tt] = MM.n
            if tt + 1 < NT4:
                emit_x3(tt + 1)
            del x3hi, x3lo

            for m2 in range(KD):
                bnk = 4 + (m2 % 2)
                lwh = {}
                for kh in range(2):
                    qb = 2 * kh
                    @block.sync
                    def _(e):
                        for qx in (qb, qb + 1):
                            sem, val = q_free[qx]
                            if sem is not None:
                                p.w(e, "sync", sem, val)
                        for semx, valx in dense_last:
                            p.w(e, "sync", semx, valx)
                        r = e.dma_start(out=WSB[:, qb * 2048: qb * 2048 + 4096],
                                        in_=pout[m2][:, kh * 4096:(kh + 1) * 4096])
                        LWh[qb].inc(r, 16)
                    lwh[kh] = (LWh[qb], LWh[qb].n)
                for kh in range(2):
                    qb = 2 * kh
                    @block.tensor
                    def _(t):
                        p.w(t, "tensor", lwh[kh][0], lwh[kh][1])
                        p.w(t, "tensor", VE, mid_ve)
                        if kh == 0:
                            bank_wait(t, bnk)
                        r = None
                        for k in range(16):
                            kk = kh * 16 + k
                            t.matmul(PS[bnk][:], WSB[:, qb * 2048 + k * 128:
                                                     qb * 2048 + (k + 1) * 128],
                                     SCRB[:, kk * 512:(kk + 1) * 512],
                                     start=(kk == 0), stop=False)
                            t.matmul(PS[bnk][:], WSB[:, qb * 2048 + k * 128:
                                                     qb * 2048 + (k + 1) * 128],
                                     SCRB[:, 16384 + kk * 512: 16384 + (kk + 1) * 512],
                                     start=False, stop=False)
                            r = t.matmul(PS[bnk][:], WSB[:, (qb + 1) * 2048 + k * 128:
                                                         (qb + 1) * 2048 + (k + 1) * 128],
                                         SCRB[:, kk * 512:(kk + 1) * 512],
                                         start=False, stop=(kh == 1 and k == 15))
                        MM.inc(r, 1)
                        q_free[qb] = (MM, MM.n)
                        q_free[qb + 1] = (MM, MM.n)
                mm_out = MM.n

                @block.vector
                def _(v):
                    p.w(v, "vector", MM, mm_out)
                    p.w(v, "vector", ST, dump_gate[0])
                    r = v.tensor_tensor(xch(m2, tt * T4, (tt + 1) * T4),
                                        PS[bnk][:],
                                        xch(m2, tt * T4, (tt + 1) * T4), Op.add)
                    VE.inc(r, 1)
                p.bank_guard[bnk] = (VE, VE.n)
            wout_ve_d[tt] = VE.n
        ln1_tt[NT4 - 1] = layer_norm(NT4 - 1, True, wout_ve_d[NT4 - 1])
        glu_done_ve = VE.n
        qkv_free_mm = MM.n
        ws_free[0] = (MM, MM.n)
        ws_free[1] = (MM, MM.n)
        dump_x("s3", glu_done_ve)

        ln1_done = VE.n
        dump_x("vv", ln1_done)

        SCRb = SCR.bitcast(BF16)
        QKVb = QKV.bitcast(BF16)
        A_HI, A_LO, B_HI, B_LO = 0, 8192, 16384, 24576

        w12_mm = {}
        xsplit_ve_d = {}
        last_pair_ve = 0

        def emit_xsplit(t8_):
            tb_ = t8_ * T8
            off = (t8_ % 2) * 4096
            @block.scalar
            def _(s):
                p.w(s, "scalar", MM, w12_mm.get(t8_ - 2, qkv_free_mm))
                p.w(s, "scalar", VE, ln1_tt[t8_ // 2])
                for k in range(KD):
                    r = s.activation(QKVb[:, off + k * T8: off + (k + 1) * T8],
                                     X[:, k * S + tb_: k * S + tb_ + T8],
                                     AF.Copy, bias=0.0)
                    SE.inc(r, 1)
            xs_se = SE.n

            @block.vector
            def _(v):
                p.w(v, "vector", SE, xs_se)
                for k in range(KD):
                    r = v.tensor_tensor(
                        QKVb[:, off + 2048 + k * T8: off + 2048 + (k + 1) * T8],
                        X[:, k * S + tb_: k * S + tb_ + T8],
                        QKVb[:, off + k * T8: off + (k + 1) * T8], Op.subtract)
                    VE.inc(r, 1)
            xsplit_ve_d[t8_] = VE.n

        emit_xsplit(0)
        for t8 in range(NT8):
            tb = t8 * T8
            xsplit_ve = xsplit_ve_d[t8]
            xoff = (t8 % 2) * 4096

            def xhi(k, _o=xoff):
                return QKVb[:, _o + k * T8: _o + (k + 1) * T8]

            def xlo(k, _o=xoff):
                return QKVb[:, _o + 2048 + k * T8: _o + 2048 + (k + 1) * T8]

            for m in range(KF):
                lw_pair = {}
                for pi, wsrc in enumerate([c1, c2]):
                    load_slab8(wsrc[m], D, pi)
                    lw_pair[pi] = ws_loaded[pi]
                b1, b2 = (0, 1) if m % 2 == 0 else (2, 3)
                for pi, bnk in [(0, b1), (1, b2)]:
                    @block.tensor
                    def _(t):
                        p.w(t, "tensor", lw_pair[pi][0], lw_pair[pi][1])
                        p.w(t, "tensor", VE, xsplit_ve)
                        bank_wait(t, bnk)
                        r = None
                        for k in range(KD):
                            t.matmul(PS[bnk][:, 0:T8], slab8(pi, k), xhi(k),
                                     start=(k == 0), stop=False)
                            r = t.matmul(PS[bnk][:, 0:T8], slab8(pi, k), xlo(k),
                                         start=False, stop=(k == KD - 1))
                        MM.inc(r, 1)
                    ws_free[pi] = (MM, MM.n)
                mm12 = MM.n

                TMPa = TMP2 if m % 2 == 0 else TMP3
                @block.scalar
                def _(s):
                    p.w(s, "scalar", MM, mm12)
                    p.w(s, "scalar", VE, last_pair_ve)
                    r = s.activation(TMP1[:, 0:T8], PS[b1][:, 0:T8], AF.Silu,
                                     scale=vec("s1", m))
                    SE.inc(r, 1)
                se_h = SE.n
                p.bank_guard[b1] = (SE, se_h)

                @block.vector
                def _(v):
                    p.w(v, "vector", SE, se_h)
                    r = v.scalar_tensor_tensor(TMPa[:, 0:T8], PS[b2][:, 0:T8],
                                               vec("s2", m), TMP1[:, 0:T8],
                                               Op.mult, Op.mult)
                    VE.inc(r, 1)
                    v.drain()
                    r = v.tensor_copy(SCRb[:, A_HI + m * T8: A_HI + (m + 1) * T8],
                                      TMPa[:, 0:T8])
                    VE.inc(r, 1)
                    v.drain()
                    r = v.tensor_tensor(SCRb[:, A_LO + m * T8: A_LO + (m + 1) * T8],
                                        TMPa[:, 0:T8],
                                        SCRb[:, A_HI + m * T8: A_HI + (m + 1) * T8],
                                        Op.subtract)
                    VE.inc(r, 1)
                p.bank_guard[b2] = (VE, VE.n - 2)
                last_pair_ve = VE.n
            h0_ve = VE.n
            w12_mm[t8] = MM.n
            if t8 + 1 < NT8:
                emit_xsplit(t8 + 1)

            def h_layer(wsrc, svec, src_hi, src_lo, dst_hi, dst_lo,
                        prev_sem, prev_val):
                nonlocal_last = [last_pair_ve]
                for m in range(KF):
                    bnk = 4 + (m % 2)
                    lwh = {}
                    for kh in range(2):
                        load_slab8(wsrc[m][:, kh * 2048:(kh + 1) * 2048], 2048, kh)
                        lwh[kh] = ws_loaded[kh]
                    for kh in range(2):
                        @block.tensor
                        def _(t):
                            p.w(t, "tensor", lwh[kh][0], lwh[kh][1])
                            p.w(t, "tensor", prev_sem, prev_val)
                            if kh == 0:
                                bank_wait(t, bnk)
                            r = None
                            for k in range(16):
                                kk = kh * 16 + k
                                t.matmul(PS[bnk][:, 0:T8], slab8(kh, k),
                                         SCRb[:, src_hi + kk * T8: src_hi + (kk + 1) * T8],
                                         start=(kk == 0), stop=False)
                                r = t.matmul(PS[bnk][:, 0:T8], slab8(kh, k),
                                             SCRb[:, src_lo + kk * T8: src_lo + (kk + 1) * T8],
                                             start=False, stop=(kk == 31))
                            MM.inc(r, 1)
                            ws_free[kh] = (MM, MM.n)
                    mmh = MM.n

                    TMPa = TMP2 if m % 2 == 0 else TMP3
                    @block.scalar
                    def _(s):
                        p.w(s, "scalar", MM, mmh)
                        p.w(s, "scalar", VE, nonlocal_last[0])
                        r = s.activation(SCRb[:, dst_hi + m * T8: dst_hi + (m + 1) * T8],
                                         PS[bnk][:, 0:T8], AF.Gelu, scale=vec(svec, m))
                        SE.inc(r, 1)
                        r = s.activation(TMPa[:, 0:T8], PS[bnk][:, 0:T8], AF.Gelu,
                                         scale=vec(svec, m))
                        SE.inc(r, 1)
                    se_pair = SE.n
                    p.bank_guard[bnk] = (SE, se_pair)

                    @block.vector
                    def _(v):
                        p.w(v, "vector", SE, se_pair)
                        r = v.tensor_tensor(SCRb[:, dst_lo + m * T8: dst_lo + (m + 1) * T8],
                                            TMPa[:, 0:T8],
                                            SCRb[:, dst_hi + m * T8: dst_hi + (m + 1) * T8],
                                            Op.subtract)
                        VE.inc(r, 1)
                    nonlocal_last[0] = VE.n
                return SE.n, VE.n

            h1_se, h1_ve = h_layer(ch1, "sh1", A_HI, A_LO, B_HI, B_LO, VE, h0_ve)
            h2_se, h2_ve = h_layer(ch2, "sh2", B_HI, B_LO, A_HI, A_LO, VE, h1_ve)

            for m2 in range(KD):
                bnk = 6 + (m2 % 2)
                lwh = {}
                for kh in range(2):
                    load_slab8(c3[m2][:, kh * 2048:(kh + 1) * 2048], 2048, kh)
                    lwh[kh] = ws_loaded[kh]
                for kh in range(2):
                    @block.tensor
                    def _(t):
                        p.w(t, "tensor", lwh[kh][0], lwh[kh][1])
                        p.w(t, "tensor", VE, h2_ve)
                        if kh == 0:
                            bank_wait(t, bnk)
                        r = None
                        for k in range(16):
                            kk = kh * 16 + k
                            t.matmul(PS[bnk][:, 0:T8], slab8(kh, k),
                                     SCRb[:, A_HI + kk * T8: A_HI + (kk + 1) * T8],
                                     start=(kk == 0), stop=False)
                            r = t.matmul(PS[bnk][:, 0:T8], slab8(kh, k),
                                         SCRb[:, A_LO + kk * T8: A_LO + (kk + 1) * T8],
                                         start=False, stop=(kk == 31))
                        MM.inc(r, 1)
                        ws_free[kh] = (MM, MM.n)
                mm3 = MM.n

                @block.vector
                def _(v):
                    p.w(v, "vector", MM, mm3)
                    p.w(v, "vector", ST, dump_gate[0])
                    r = v.scalar_tensor_tensor(X[:, m2 * S + tb: m2 * S + tb + T8], PS[bnk][:, 0:T8], vec("s3", m2), X[:, m2 * S + tb: m2 * S + tb + T8], Op.mult, Op.add)
                    VE.inc(r, 1)
                p.bank_guard[bnk] = (VE, VE.n)
        ffn_done_ve = VE.n
        dump_x("ww", ffn_done_ve)

        out_m = {}
        for tt in range(NT4):
            layer_norm(tt, False, ffn_done_ve,
                       per_m=(out_m if tt == NT4 - 1 else None))
        ln2_done = VE.n

        @block.gpsimd
        def _(g):
            for m in range(KD):
                p.w(g, "gpsimd", VE, out_m[m])
                r = g.dma_start(out=yT[m], in_=xch(m))
                ST.inc(r, 16)
            g.wait_ge(ST.h, ST.n)

    return nc


_NC_CACHE = None


def _get_nc():
    global _NC_CACHE
    if _NC_CACHE is None:
        _NC_CACHE = build_program()
    return _NC_CACHE


def _tile_w(wmat):
    M, K = wmat.shape
    w4 = wmat.reshape(M // 128, 128, K // 128, 128)
    return np.ascontiguousarray(w4.transpose(0, 3, 2, 1)).reshape(M // 128, 128, K)


def _vec_tile(v, n):
    return np.ascontiguousarray(np.asarray(v, np.float32).reshape(n, 128).T)


def kernel(**inp):
    x = np.asarray(inp["L0_attn_x"], dtype=np.float32)
    B = x.shape[0]

    weights = {
        "wg": _tile_w(np.asarray(inp["L0_attn_Wg"], np.float32)),
        "wv": _tile_w(np.asarray(inp["L0_attn_Wv"], np.float32)),
        "wd": _tile_w(np.asarray(inp["L0_attn_Wd"], np.float32)),
        "wqkv": _tile_w(np.asarray(inp["L0_attn_Wqkv"], np.float32)),
        "wo": _tile_w(np.asarray(inp["L0_attn_Wo"], np.float32)),
        "wok": np.ascontiguousarray(
            np.asarray(inp["L0_attn_Wo"], np.float32)
            .reshape(8, 128, 8, 128).transpose(2, 3, 0, 1)).reshape(8, 128, D),
        "wgate": _tile_w(np.asarray(inp["L0_attn_Wgate"], np.float32)),
        "wup": _tile_w(np.asarray(inp["L0_attn_Wup"], np.float32)),
        "wout": _tile_w(np.asarray(inp["L0_attn_Wout"], np.float32)),
        "f1": _tile_w(np.asarray(inp["L0_ffn_w1_w"], np.float32)),
        "f2": _tile_w(np.asarray(inp["L0_ffn_w2_w"], np.float32)),
        "fh1": _tile_w(np.asarray(inp["L0_ffn_h1_w"], np.float32)),
        "fh2": _tile_w(np.asarray(inp["L0_ffn_h2_w"], np.float32)),
        "f3": _tile_w(np.asarray(inp["L0_ffn_w3_w"], np.float32)),
    }

    vecs = np.zeros((128, VECN), np.float32)
    for name, arr in [("bg", inp["L0_attn_bg"]), ("bv", inp["L0_attn_bv"]),
                      ("bd", inp["L0_attn_bd"]), ("wm", inp["L0_attn_w_rms_mix"]),
                      ("wa", inp["L0_attn_w_rms_attn"]), ("wf", inp["L0_attn_w_rms_ffn"]),
                      ("s1", inp["L0_ffn_w1_s"]), ("s2", inp["L0_ffn_w2_s"]),
                      ("sh1", inp["L0_ffn_h1_s"]), ("sh2", inp["L0_ffn_h2_s"]),
                      ("s3", inp["L0_ffn_w3_s"]),
                      ("eps1", np.full(128, RMS_EPS)), ("eps2", np.full(128, LN_EPS))]:
        a, b = VC[name]
        vecs[:, a:b] = _vec_tile(arr, b - a)

    consts = np.zeros((128, CONN), np.float32)
    consts[:, CC["ident"][0]:CC["ident"][1]] = np.eye(128, dtype=np.float32)
    consts[:, CC["ones"][0]:CC["ones"][1]] = 1.0
    j = np.arange(128)[:, None]
    kidx = np.arange(256)[None, :]
    valid = (kidx <= 128 + j) & (kidx >= j + 1)
    consts[:, CC["mask"][0]:CC["mask"][1]] = np.where(valid, 0.0, NEG)
    consts[:, CC["mask0"][0]:CC["mask0"][1]] = np.where(valid & (kidx >= 128), 0.0, NEG)

    in_maps = []
    for b in range(B):
        m = dict(weights)
        m["vecs"] = vecs
        m["consts"] = consts
        m["xT"] = np.ascontiguousarray(x[b].T).reshape(KD, 128, S)
        in_maps.append(m)

    nc = _get_nc()
    res = run_bass_kernel_spmd(nc, in_maps, core_ids=list(range(B)))
    out = np.empty((B, S, D), np.float32)
    for b in range(B):
        out[b] = np.asarray(res.results[b]["yT"]).reshape(D, S).T
    return out
